# version 19 (final)
# speedup vs baseline: 1.0752x; 1.0066x over previous
import numpy as np
import ml_dtypes
import concourse.bass as bass
import concourse.mybir as mybir
from contextlib import ExitStack

F32 = mybir.dt.float32
BF16 = mybir.dt.bfloat16
I32 = mybir.dt.int32
AF = mybir.ActivationFunctionType
ALU = mybir.AluOpType
AX = mybir.AxisListType
BF16NP = ml_dtypes.bfloat16

TENSOR, VECTOR, SCALAR, SYNC, GPSIMD = "tensor", "vector", "scalar", "sync", "gpsimd"
ENGINES = [TENSOR, VECTOR, SCALAR, SYNC, GPSIMD]
DMA_STREAMS = {SYNC: 16, GPSIMD: 16}


class Buf:
    __slots__ = ("ap", "wr", "rd")
    def __init__(self, ap=None):
        self.ap = ap
        self.wr = None
        self.rd = []


class Prog:
    def __init__(self, name="prog", n_extra_dma_sems=6):
        self.nc = bass.Bass("TRN2", target_bir_lowering=False, debug=False)
        self.es = ExitStack()
        self.name = name
        self.ops = {e: [] for e in ENGINES}
        self.cnt = {e: 0 for e in ENGINES}
        self.const_arrays = {}
        self._const_n = 0
        self._finalized = False
        self.extra_sems = {}
        self._sem_requests = set()
        self._fence = []
        self._rr = {}
        self.NDMASEMS = 6
        self._const_aps = {}
        self._scopes = []
        self._names_used = set()

    def dram_in(self, name, shape, dtype=F32):
        return self.nc.dram_tensor(name, list(shape), dtype, kind="ExternalInput").ap()

    def dram_out(self, name, shape, dtype=F32):
        return self.nc.dram_tensor(name, list(shape), dtype, kind="ExternalOutput").ap()

    def dram_tmp(self, name, shape, dtype=F32):
        return self.nc.dram_tensor(name, list(shape), dtype, kind="Internal").ap()

    def const(self, arr, name=None):
        arr = np.ascontiguousarray(arr)
        if name is None:
            name = f"c{self._const_n}"
            self._const_n += 1
        if name in self._const_aps:
            old = self.const_arrays[name]
            assert old.shape == arr.shape and old.dtype == arr.dtype and np.array_equal(old, arr), name
            return self._const_aps[name]
        dt = {np.dtype(np.float32): F32, np.dtype(BF16NP): BF16,
              np.dtype(np.int32): I32}[arr.dtype]
        self.const_arrays[name] = arr
        ap = self.dram_in(name, arr.shape, dt)
        self._const_aps[name] = ap
        return ap

    def _uniq(self, name):
        if name in self._names_used:
            i = 2
            while f"{name}_{i}" in self._names_used:
                i += 1
            name = f"{name}_{i}"
        self._names_used.add(name)
        return name

    def sb(self, name, shape, dtype):
        tgt = self._scopes[-1] if self._scopes else self.es
        return tgt.enter_context(self.nc.sbuf_tensor(self._uniq(name), list(shape), dtype))

    def ps(self, name, shape, dtype=F32):
        tgt = self._scopes[-1] if self._scopes else self.es
        return tgt.enter_context(self.nc.psum_tensor(self._uniq(name), list(shape), dtype))

    def push_scope(self):
        self._scopes.append(ExitStack())

    def pop_scope(self):
        self._scopes.pop().close()
        self.fence()

    def op(self, engine, emit, r=(), w=(), sem=None, inc=None, deps=(), is_dma=False):
        assert not self._finalized
        if inc is None:
            inc = 16 if (engine in DMA_STREAMS or is_dma) else 1
        waits = {}
        def add(tok):
            if tok is None:
                return
            s, v = tok
            if v <= 0:
                return
            if waits.get(s, 0) < v:
                waits[s] = v
        for d in deps:
            add(d)
        for d in self._fence:
            add(d)
        for b in r:
            add(b.wr)
        for b in w:
            add(b.wr)
            for t in b.rd:
                add(t)
        if engine in DMA_STREAMS or is_dma:
            if inc == 1:
                inc = 16
            key = engine if not is_dma else engine + "_dma"
            if sem is None:
                rr = self._rr.get(key, 0)
                self._rr[key] = (rr + 1) % self.NDMASEMS
                stream = f"{key}{rr}"
            else:
                stream = sem
            add((stream, self.cnt.get(stream, 0)))
        else:
            stream = sem if sem is not None else engine
        if sem is not None:
            self._sem_requests.add(sem)
        self.cnt.setdefault(stream, 0)
        self.cnt[stream] += inc
        tok = (stream, self.cnt[stream])
        self.ops[engine].append((emit, waits, tok, inc))
        for b in r:
            b.rd.append(tok)
        for b in w:
            b.wr = tok
            b.rd = []
        return tok

    def barrier_token(self):
        return dict(self.cnt)

    def fence(self):
        self._fence = [(s, v) for s, v in self.cnt.items()]

    def scratch(self):
        return ExitStack()

    def finalize(self):
        assert not self._finalized
        self._finalized = True
        nc = self.nc
        sems = {}
        for s in list(self.cnt.keys()):
            sems[s] = self.es.enter_context(nc.semaphore(f"sem_{s}"))
        self._sems = sems
        block = self.es.enter_context(nc.Block())
        eng_objs = {}
        def make_section(eng_name):
            dec = getattr(block, eng_name)
            def body(eng):
                satisfied = {}
                for emit, waits, tok, inc in self.ops[eng_name]:
                    for s, v in sorted(waits.items()):
                        if eng_name == TENSOR and s == "tensor":
                            continue
                        if s == tok[0]:
                            if satisfied.get(s, 0) >= v:
                                continue
                            eng.wait_ge(sems[s], v)
                            satisfied[s] = v
                        else:
                            if satisfied.get(s, 0) >= v:
                                continue
                            eng.wait_ge(sems[s], v)
                            satisfied[s] = v
                    inst = emit(eng)
                    if inst is not None:
                        inst.then_inc(sems[tok[0]], inc)
                    else:
                        eng.nop().then_inc(sems[tok[0]], inc)
            dec(body)
        for e in ENGINES:
            if self.ops[e]:
                make_section(e)

    def close(self):
        self.es.close()

    def in_maps(self, per_core_inputs):
        return [{**m, **self.const_arrays} for m in per_core_inputs]



def dma(dst, src, **kw):
    return lambda e: e.dma_start(out=dst, in_=src, **kw)

def dma_tp(dst, src):
    return lambda e: e.dma_start_transpose(dst, src)

def mm(out, lhsT, rhs, start=True, stop=True, **kw):
    return lambda e: e.matmul(out, lhsT, rhs, start=start, stop=stop, **kw)

def tp(out, in_, ident, **kw):
    return lambda e: e.transpose(out, in_, ident, **kw)

def act(out, in_, func, **kw):
    return lambda e: e.activation(out, in_, func, **kw)

def tcopy(out, in_):
    return lambda e: e.tensor_copy(out, in_)

def memset(ap_, v):
    return lambda e: e.memset(ap_, v)

def tt(out, a, b, op):
    return lambda e: e.tensor_tensor(out, a, b, op)

def ts(out, a, s1, s2=None, op0=ALU.mult, op1=ALU.bypass, **kw):
    if s2 is None:
        return lambda e: e.tensor_single_scalar(out, a, s1, op0)
    return lambda e: e.tensor_scalar(out, a, s1, s2, op0, op1, **kw)

def stt(out, in0, scalar, in1, op0, op1):
    return lambda e: e.scalar_tensor_tensor(out, in0, scalar, in1, op0, op1)

def red(out, in_, op=ALU.add, absv=False, negate=False):
    return lambda e: e.tensor_reduce(out, in_, AX.X, op,
                                     apply_absolute_value=absv, negate=negate)

def scan(out, d0, d1, init, op0=ALU.mult, op1=ALU.add):
    return lambda e: e.tensor_tensor_scan(out, d0, d1, init, op0, op1)

def recip(out, in_):
    return lambda e: e.reciprocal(out, in_)


import numpy as np

D_l0attn = 1024
S_l0attn = 2048
H_l0attn = 4
DH_l0attn = 256
NKT_l0attn = D_l0attn // 128
NTS_l0attn = S_l0attn // 128
NCH_l0attn = S_l0attn // 512


def build_l0_attn(p: Prog, x_in, x_out, w, nseq=2, dbg=None, stage=99, out_xmax=None):
    nc = p.nc

    ident_bf = p.const(np.eye(128, dtype=BF16NP), "ident_bf")
    ident_f32 = p.const(np.eye(128, dtype=np.float32), "ident_f32")
    ones_row_bf = p.const(np.ones((1, 128), BF16NP), "ones_row_bf")
    sl_mask = p.const((-np.tril(np.ones((128, 128), np.float32), -1)).astype(BF16NP), "sl_mask")
    su_mask = p.const(np.triu(np.ones((128, 128), np.float32), 0).astype(BF16NP), "su_mask")
    cc = np.zeros((128, 7), np.float32)
    cc[:, 0] = 1e-5; cc[:, 1] = -0.5; cc[:, 2] = 1e-6
    cc[:, 3] = 1.0; cc[:, 4] = -1.0; cc[:, 5] = 1e-5; cc[:, 6] = 1.0 / 256.0
    cvec = p.const(cc, "ln_consts_l0")
    ones_col_c = p.const(np.ones((128, 1), BF16NP), "ones_col_c")
    ones_row_f = p.const(np.ones((1, 128), np.float32), "ones_row_f")

    sb_id = p.sb("sb_id", [128, 128], BF16); b_id = Buf()
    sb_idf = p.sb("sb_idf", [128, 128], F32); b_idf = Buf()
    sb_onesb = p.sb("sb_onesb", [1, 128], BF16); b_onesb = Buf()
    sb_sl = p.sb("sb_sl", [128, 128], BF16); b_sl = Buf()
    sb_su = p.sb("sb_su", [128, 128], BF16); b_su = Buf()
    consts = p.sb("consts", [128, 7], F32); bconsts = Buf()
    sb_onescol = p.sb("sb_onescol", [128, 1], BF16); b_onescol = Buf()
    sb_onesrf = p.sb("sb_onesrf", [1, 128], F32); b_onesrf = Buf()
    sb_ones4 = p.sb("sb_ones4", [65, 128], BF16); b_ones4 = Buf()
    EPS = consts[:, 0:1]; NHALF = consts[:, 1:2]; EPSL2 = consts[:, 2:3]
    ONE = consts[:, 3:4]; NEG1 = consts[:, 4:5]; EPSRMS = consts[:, 5:6]
    for dst, src, bb in ((sb_id, ident_bf, b_id), (sb_idf, ident_f32, b_idf),
                         (sb_onesb, ones_row_bf, b_onesb), (sb_sl, sl_mask, b_sl),
                         (sb_su, su_mask, b_su), (consts, cvec, bconsts),
                         (sb_onescol, ones_col_c, b_onescol), (sb_onesrf, ones_row_f, b_onesrf)):
        p.op(SYNC, dma(dst[:], src), w=[bb])

    stats = p.sb("stats", [128, 16], F32); bstats2 = [Buf(), Buf()]; bstats = bstats2[0]
    xnm = p.sb("xnm", [128, 2], F32); bxnm = Buf()
    sxm = p.sb("sxm", [1, 16], F32); bsxm = Buf()
    scr = p.sb("scr", [128, D_l0attn], F32); bscr = Buf()
    xt0 = p.sb("xt0", [128, D_l0attn], F32)
    _bx = Buf(); bxt = [_bx, _bx]; xt1 = xt0
    xn_tok = p.sb("xn_tok", [128, D_l0attn], BF16); bxn_tok = Buf()
    out_t = p.sb("out_t", [128, D_l0attn], F32); bout_t = Buf()
    ln2o = p.sb("ln2o", [128, D_l0attn], F32); bln2o = Buf()

    psP = [p.ps(f"psP{i}", [128, 512], F32) for i in range(4)]
    bpsP = [Buf() for _ in range(4)]
    psA, bpsA = psP[0][:, 0:512], bpsP[0]
    psB, bpsB = psP[1][:, 0:512], bpsP[1]
    psS = p.ps("psS", [128, 512], F32); bpsS = Buf()
    psK = psP[2][:, 0:128]; bpsK = bpsP[2]
    psO = psP[2][:, 128:384]; bpsO = bpsP[2]
    psU = psP[3][:, 0:256]; bpsU = bpsP[3]
    psT2t = [p.ps(f"psT{i}", [128, 128], BF16) for i in range(2)]
    bpsT2 = [Buf(), Buf()]
    psT = psT2t[0]; bpsT = bpsT2[0]

    A_xn = p.sb("A_xn", [128, NKT_l0attn * S_l0attn], BF16); bA_xn = Buf()
    A_oall = p.sb("A_oall", [128, NKT_l0attn * S_l0attn], BF16); bA_oall = Buf()
    A_qf = p.sb("A_qf", [128, 2 * S_l0attn], BF16); bA_qf = Buf()
    A_kf = p.sb("A_kf", [128, 2 * S_l0attn], BF16); bA_kf = Buf()
    A_vf = p.sb("A_vf", [128, 2 * S_l0attn], BF16); bA_vf = Buf()
    A_raw = p.sb("A_raw", [128, 2 * S_l0attn], BF16); bA_raw = Buf()
    A_ema = A_raw; bA_ema = bA_raw
    A_kntok = p.sb("A_kntok", [128, NTS_l0attn * DH_l0attn], BF16); bA_kntok = Buf()
    A_kbtok = p.sb("A_kbtok", [128, NTS_l0attn * DH_l0attn], BF16); bA_kbtok = Buf()
    A_vbtok = p.sb("A_vbtok", [128, NTS_l0attn * DH_l0attn], BF16); bA_vbtok = Buf()
    A_w = p.sb("A_w", [128, 8 * DH_l0attn], BF16); bA_w = Buf()
    A_wo = p.sb("A_wo", [128, 8 * 1024], BF16); bA_wo = Buf()
    A_raw2 = p.sb("A_raw2", [128, 2 * 15 * 128], BF16); bA_raw2 = Buf()
    A_gbc = A_kntok[:, 0:S_l0attn]; bA_gbc = bA_kntok
    A_d1 = A_kbtok; bA_d1 = bA_kbtok
    A_mbc = A_kntok[:, S_l0attn:2 * S_l0attn]; bA_mbc = bA_kntok

    bsp = p.sb("bsp", [65, S_l0attn], BF16); b_bsp = Buf()
    gsp = p.sb("gsp", [65, S_l0attn], BF16); b_gsp = Buf()
    msp = p.sb("msp", [65, S_l0attn], BF16); b_msp = Buf()
    bspB = p.sb("bspB", [1, S_l0attn], BF16); b_bspB = Buf()
    gspB = p.sb("gspB", [1, S_l0attn], BF16); b_gspB = Buf()
    mspB = p.sb("mspB", [1, S_l0attn], BF16); b_mspB = Buf()
    btok = p.sb("btok", [128, 66 * 16], F32); b_btok = Buf()
    wsp = p.sb("wsp", [128, 8 * 66 * 3], BF16); b_wsp = Buf()
    wsp_src = p.sb("wsp_src", [4, 1024], F32); b_wspsrc = Buf()
    conv_w = p.sb("conv_w", [128, 3 * 8 * 4], F32); b_convw = Buf()
    onw = p.sb("onw", [128, 2], F32); b_onw = Buf()
    wnat = p.sb("wnat", [128, 1024], BF16); b_wnat = Buf()

    L_sb = A_raw[:, 22 * 128:23 * 128]; bL = bA_raw
    N_sb = A_raw[:, 23 * 128:24 * 128]; bN = bA_raw
    lad = A_raw[:, 0:11 * 128]; blad = bA_raw
    Q_sb = A_raw[:, 20 * 128:22 * 128]; bQ = bA_raw
    aT_sb = A_raw[:, 19 * 128:20 * 128]; baT = bA_raw
    wTn = A_raw[:, 11 * 128:13 * 128]; bwTn = bA_raw
    unew = A_raw[:, 13 * 128:15 * 128]; bunew = bA_raw
    S_sb = A_raw[:, 15 * 128:19 * 128]; bS = bA_raw
    rsrow = scr[0:1, 0:512]; brsrow = bscr
    rmsrow = scr[0:1, 512:1024]; brmsrow = bscr

    def ln_stats(src_ap, bsrc, sl=0):
        c = 8 * sl
        bst = bstats2[sl]
        p.op(VECTOR, red(stats[:, c + 0:c + 1], src_ap), r=[bsrc], w=[bst])
        p.op(SCALAR, act(scr[:], src_ap, AF.Square, accum_out=stats[:, c + 1:c + 2]),
             r=[bsrc], w=[bst])
        p.op(VECTOR, ts(stats[:, c + 2:c + 3], stats[:, c + 0:c + 1], 1.0 / D_l0attn),
             r=[bst], w=[bst])
        p.op(VECTOR, ts(stats[:, c + 3:c + 4], stats[:, c + 2:c + 3], -1.0),
             r=[bst], w=[bst])
        p.op(VECTOR, ts(stats[:, c + 4:c + 5], stats[:, c + 1:c + 2], 1.0 / D_l0attn),
             r=[bst], w=[bst])
        p.op(VECTOR, stt(stats[:, c + 5:c + 6], stats[:, c + 2:c + 3], stats[:, c + 3:c + 4],
                         stats[:, c + 4:c + 5], ALU.mult, ALU.add), r=[bst], w=[bst])
        p.op(SCALAR, act(stats[:, c + 6:c + 7], stats[:, c + 5:c + 6], AF.Ln, bias=EPS),
             r=[bst, bconsts], w=[bst])
        p.op(SCALAR, act(stats[:, c + 7:c + 8], stats[:, c + 6:c + 7], AF.Exp, scale=NHALF),
             r=[bst, bconsts], w=[bst])
        return stats[:, c + 2:c + 3], stats[:, c + 7:c + 8]

    def dump(name, ap_, rbufs):
        if dbg is not None and name in dbg:
            p.op(GPSIMD, dma(dbg[name], ap_), r=rbufs, sem="dstore")
    p.op(VECTOR, memset(sb_ones4[:], 1.0), w=[b_ones4])
    p.op(GPSIMD, dma(conv_w[:, 0:32].rearrange("p (kt j) -> p kt j", kt=8),
                     w["wq_conv"].rearrange("(kt p) j -> p kt j", p=128)), w=[b_convw])
    p.op(GPSIMD, dma(conv_w[:, 32:64].rearrange("p (kt j) -> p kt j", kt=8),
                     w["wk_conv"].rearrange("(kt p) j -> p kt j", p=128)), w=[b_convw])
    p.op(GPSIMD, dma(conv_w[:, 64:96].rearrange("p (kt j) -> p kt j", kt=8),
                     w["wv_conv"].rearrange("(kt p) j -> p kt j", p=128)), w=[b_convw])
    p.op(GPSIMD, dma(onw[:], w["onorm_w"].rearrange("(s p) -> p s", p=128),
                     allow_slow_non_contiguous=True), w=[b_onw])
    mixb = p.sb("mixb", [65, 1], F32); b_mixb = Buf()
    mixbB = p.sb("mixbB", [1, 1], F32); b_mixbB = Buf()
    p.op(VECTOR, memset(mixb[:], 0.0), w=[b_mixb])
    p.op(GPSIMD, dma(mixb[0:65:32, 0:1], w["mix_bias"][0:3].rearrange("(f one) -> f one", one=1),
                     allow_slow_non_contiguous=True), w=[b_mixb])
    p.op(GPSIMD, dma(mixbB[:], w["mix_bias"][3:4].rearrange("(f one) -> f one", one=1),
                     allow_slow_non_contiguous=True), w=[b_mixbB])
    p.op(VECTOR, memset(wsp[:], 0.0), w=[b_wsp])
    for wi, wnm in enumerate(("Wb", "Wdec", "Wmix")):
        p.op(SYNC, dma(wsp_src[:, 0:1024], w[wnm]), w=[b_wspsrc])
        for kt in range(NKT_l0attn):
            p.op(TENSOR, tp(psK[0:128, 0:4], wsp_src[:, kt * 128:(kt + 1) * 128],
                            sb_idf[0:4, 0:4]), r=[b_wspsrc, b_idf], w=[bpsK])
            base = (wi * 8 + kt) * 66
            p.op(VECTOR, tcopy(wsp[:, base: base + 65: 32], psK[:, 0:3]),
                 r=[bpsK], w=[b_wsp])
            p.op(VECTOR, tcopy(wsp[:, base + 65: base + 66], psK[:, 3:4]),
                 r=[bpsK], w=[b_wsp])
    for mt in range(8):
        p.op(GPSIMD, dma(wnat[:], w["Wo"][mt * 128:(mt + 1) * 128, :]), w=[b_wnat])
        for kt in range(8):
            pt, bpt = psT2t[kt % 2], bpsT2[kt % 2]
            p.op(TENSOR, tp(pt[0:128, :], wnat[:, kt * 128:(kt + 1) * 128], sb_id[:]),
                 r=[b_wnat, b_id], w=[bpt])
            if kt % 2 == 0:
                p.op(VECTOR, tcopy(A_wo[:, kt * 1024 + mt * 128: kt * 1024 + (mt + 1) * 128],
                                   pt[0:128, :]), r=[bpt], w=[bA_wo])
            else:
                p.op(SCALAR, act(A_wo[:, kt * 1024 + mt * 128: kt * 1024 + (mt + 1) * 128],
                                 pt[0:128, :], AF.Identity), r=[bpt], w=[bA_wo])

    for b in range(nseq):
        for t in range(NTS_l0attn):
            xt = xt0
            bx = bxt[0]
            p.op(SYNC, dma(xt[:], x_in[b, t * 128:(t + 1) * 128, :]), w=[bx])
            ln_stats(xt[:], bx)
            p.op(VECTOR, ts(xn_tok[:], xt[:], stats[:, 2:3], stats[:, 7:8],
                            ALU.subtract, ALU.mult), r=[bx, bstats], w=[bxn_tok])
            for kt in range(NKT_l0attn):
                pt, bpt = psT2t[kt % 2], bpsT2[kt % 2]
                p.op(TENSOR, tp(pt[0:128, :], xn_tok[:, kt * 128:(kt + 1) * 128], sb_id[:]),
                     r=[bxn_tok, b_id], w=[bpt])
                dst = A_xn[:, kt * S_l0attn + t * 128: kt * S_l0attn + (t + 1) * 128]
                if kt % 2 == 0:
                    p.op(VECTOR, tcopy(dst, pt[0:128, :]), r=[bpt], w=[bA_xn])
                else:
                    p.op(SCALAR, act(dst, pt[0:128, :], AF.Identity), r=[bpt], w=[bA_xn])

        if b == 0:
            dump("xnT", A_xn[:, 0:256], [bA_xn])
        if stage < 2:
            continue
        for wi, (dstsp, bdst, dstB, bdstB) in enumerate((
                (bsp, b_bsp, bspB, b_bspB), (gsp, b_gsp, gspB, b_gspB),
                (msp, b_msp, mspB, b_mspB))):
            for ch in range(NCH_l0attn):
                pspick, bps = (psA, bpsA) if ch % 2 == 0 else (psB, bpsB)
                ps2, bps2 = (psB, bpsB) if ch % 2 == 0 else (psA, bpsA)
                for kt in range(NKT_l0attn):
                    base = (wi * 8 + kt) * 66
                    p.op(TENSOR, mm(pspick[0:65, :], wsp[:, base: base + 65],
                                    A_xn[:, kt * S_l0attn + ch * 512: kt * S_l0attn + (ch + 1) * 512],
                                    start=(kt == 0), stop=(kt == NKT_l0attn - 1)),
                         r=[b_wsp, bA_xn], w=[bps])
                    p.op(TENSOR, mm(ps2[0:1, :], wsp[:, base + 65: base + 66],
                                    A_xn[:, kt * S_l0attn + ch * 512: kt * S_l0attn + (ch + 1) * 512],
                                    start=(kt == 0), stop=(kt == NKT_l0attn - 1)),
                         r=[b_wsp, bA_xn], w=[bps2])
                if wi == 2:
                    p.op(SCALAR, act(dstsp[:, ch * 512:(ch + 1) * 512], pspick[0:65, :],
                                     AF.Sigmoid, bias=mixb[:]), r=[bps, b_mixb], w=[bdst])
                    p.op(SCALAR, act(dstB[:, ch * 512:(ch + 1) * 512], ps2[0:1, :],
                                     AF.Sigmoid, bias=mixbB[:]), r=[bps2, b_mixbB], w=[bdstB])
                else:
                    p.op(SCALAR, act(dstsp[:, ch * 512:(ch + 1) * 512], pspick[0:65, :],
                                     AF.Sigmoid), r=[bps], w=[bdst])
                    p.op(SCALAR, act(dstB[:, ch * 512:(ch + 1) * 512], ps2[0:1, :],
                                     AF.Sigmoid), r=[bps2], w=[bdstB])
        for t in range(NTS_l0attn):
            p.op(TENSOR, tp(psT[:, 0:65], bsp[:, t * 128:(t + 1) * 128], sb_id[0:65, 0:65]),
                 r=[b_bsp, b_id], w=[bpsT])
            p.op(TENSOR, tp(psT[:, 66:67], bspB[:, t * 128:(t + 1) * 128], sb_id[0:1, 0:1]),
                 r=[b_bspB, b_id], w=[bpsT])
            p.op(VECTOR, tcopy(btok[:, t * 66:t * 66 + 65], psT[:, 0:65]),
                 r=[bpsT], w=[b_btok])
            p.op(VECTOR, tcopy(btok[:, t * 66 + 65:t * 66 + 66], psT[:, 66:67]),
                 r=[bpsT], w=[b_btok])

        if b == 0:
            dump("bsp", bsp[0:65, 0:256], [b_bsp])
            dump("gsp", gsp[0:65, 0:256], [b_gsp])
            dump("msp", msp[0:65, 0:256], [b_msp])
            dump("btok", btok[:, 0:66], [b_btok])
        for h in range(H_l0attn if stage >= 3 else 0):
            bcol = 32 * h if h < 3 else 65
            diag = A_oall[:, (2 * h) * S_l0attn + 512:(2 * h) * S_l0attn + 1024]
            b_diag = bA_oall
            for ti, (wname, cwoff, dstA, bdstA) in enumerate(
                    (("Wq", 0, A_qf, bA_qf), ("Wk", 32, A_kf, bA_kf),
                     ("Wv", 64, A_vf, bA_vf))):
                for mt in range(2):
                    p.op(GPSIMD, dma(wnat[:], w[wname][h * 256 + mt * 128: h * 256 + (mt + 1) * 128, :]),
                         w=[b_wnat])
                    for kt in range(8):
                        pt, bpt = psT2t[kt % 2], bpsT2[kt % 2]
                        p.op(TENSOR, tp(pt[0:128, :], wnat[:, kt * 128:(kt + 1) * 128], sb_id[:]),
                             r=[b_wnat, b_id], w=[bpt])
                        if kt % 2 == 0:
                            p.op(VECTOR, tcopy(A_w[:, kt * DH_l0attn + mt * 128: kt * DH_l0attn + (mt + 1) * 128],
                                               pt[0:128, :]), r=[bpt], w=[bA_w])
                        else:
                            p.op(SCALAR, act(A_w[:, kt * DH_l0attn + mt * 128: kt * DH_l0attn + (mt + 1) * 128],
                                             pt[0:128, :], AF.Identity), r=[bpt], w=[bA_w])
                for mt in range(2):
                    for kt in range(NKT_l0attn):
                        for ch in range(NCH_l0attn):
                            p.op(TENSOR, mm(psP[ch][:, 0:512],
                                            A_w[:, kt * DH_l0attn + mt * 128: kt * DH_l0attn + (mt + 1) * 128],
                                            A_xn[:, kt * S_l0attn + ch * 512: kt * S_l0attn + (ch + 1) * 512],
                                            start=(kt == 0), stop=(kt == NKT_l0attn - 1)),
                                 r=[bA_w, bA_xn], w=[bpsP[ch]])
                    for ch in range(NCH_l0attn):
                        dst = A_raw[:, mt * S_l0attn + ch * 512: mt * S_l0attn + (ch + 1) * 512]
                        if ch % 2 == 0:
                            p.op(VECTOR, tcopy(dst, psP[ch][:, 0:512]), r=[bpsP[ch]], w=[bA_raw])
                        else:
                            p.op(SCALAR, act(dst, psP[ch][:, 0:512], AF.Identity),
                                 r=[bpsP[ch]], w=[bA_raw])
                for mt in range(2):
                    ktg = h * 2 + mt
                    for j in range(4):
                        p.op(VECTOR, ts(diag[:, j * 128:(j + 1) * 128], sb_id[:],
                                        conv_w[:, cwoff + ktg * 4 + j: cwoff + ktg * 4 + j + 1],
                                        None, ALU.mult), r=[b_id, b_convw], w=[b_diag])
                    for ch in range(NCH_l0attn):
                        base = mt * S_l0attn + ch * 512
                        p.op(TENSOR, mm(psP[ch][:, 0:512], diag[:, 3 * 128:4 * 128],
                                        A_raw[:, base: base + 512],
                                        start=True, stop=False), r=[b_diag, bA_raw],
                             w=[bpsP[ch]])
                    for j in range(3):
                        sh = 3 - j
                        for ch in range(NCH_l0attn):
                            base = mt * S_l0attn + ch * 512
                            if ch == 0:
                                p.op(TENSOR, mm(psP[ch][:, sh:512], diag[:, j * 128:(j + 1) * 128],
                                                A_raw[:, base: base + 512 - sh],
                                                start=False, stop=(j == 2)),
                                     r=[b_diag, bA_raw], w=[bpsP[ch]])
                            else:
                                p.op(TENSOR, mm(psP[ch][:, 0:512], diag[:, j * 128:(j + 1) * 128],
                                                A_raw[:, base - sh: base + 512 - sh],
                                                start=False, stop=(j == 2)),
                                     r=[b_diag, bA_raw], w=[bpsP[ch]])
                    for ch in range(NCH_l0attn):
                        p.op(SCALAR, act(dstA[:, mt * S_l0attn + ch * 512: mt * S_l0attn + (ch + 1) * 512],
                                         psP[ch][:, 0:512], AF.Silu), r=[bpsP[ch]], w=[bdstA])

            if b == 0 and h == 0:
                dump("qf_raw", A_qf[:, 0:256], [bA_qf])
                dump("kf_raw", A_kf[:, 0:256], [bA_kf])
                dump("vf", A_vf[:, 0:256], [bA_vf])
            if stage < 4:
                continue
            sqscr = A_oall[:, (2 * h) * S_l0attn:(2 * h) * S_l0attn + S_l0attn]
            bsqscr = bA_oall
            def l2norm_feat(srcA, bsrcA, toksave=None):
                for ch in range(NCH_l0attn):
                    pspick, bps = (psA, bpsA) if ch % 2 == 0 else (psB, bpsB)
                    for mt in range(2):
                        p.op(SCALAR, act(sqscr[:, 0:512],
                                         srcA[:, mt * S_l0attn + ch * 512: mt * S_l0attn + (ch + 1) * 512],
                                         AF.Square), r=[bsrcA], w=[bsqscr])
                        p.op(TENSOR, mm(psS[0:1, 0:512], sb_onescol[:], sqscr[:, 0:512],
                                        start=(mt == 0), stop=(mt == 1)),
                             r=[bsqscr, b_onescol], w=[bpsS])
                    p.op(SCALAR, act(rsrow, psS[0:1, 0:512],
                                     AF.Ln, bias=EPSL2[0:1, :]), r=[bpsS, bconsts],
                         w=[brsrow])
                    p.op(SCALAR, act(rsrow, rsrow,
                                     AF.Exp, scale=NHALF[0:1, :]), r=[brsrow, bconsts],
                         w=[brsrow])
                    if b == 0 and h == 0 and ch == 2 and srcA is A_qf:
                        dump("rs_ch2", rsrow, [brsrow])
                    p.op(TENSOR, mm(pspick, sb_onesrf[:], rsrow,
                                    start=True, stop=True), r=[b_onesrf, brsrow], w=[bps])
                    for mt in range(2):
                        p.op(VECTOR, tt(srcA[:, mt * S_l0attn + ch * 512: mt * S_l0attn + (ch + 1) * 512],
                                        srcA[:, mt * S_l0attn + ch * 512: mt * S_l0attn + (ch + 1) * 512],
                                        pspick, ALU.mult), r=[bps, bsrcA], w=[bsrcA])
            l2norm_feat(A_qf, bA_qf)
            l2norm_feat(A_kf, bA_kf)

            if b == 0 and h == 0:
                dump("qn", A_qf[:, 0:256], [bA_qf])
                dump("kn", A_kf[:, 0:256], [bA_kf])
                dump("qn_l", A_qf[:, 1024:1280], [bA_qf])
                dump("kn_l", A_kf[:, 1024:1280], [bA_kf])
            if stage < 5:
                continue
            for t in range(NTS_l0attn):
                for mt in range(2):
                    pt, bpt = psT2t[0], bpsT2[0]
                    p.op(TENSOR, tp(pt[0:128, :], A_kf[:, mt * S_l0attn + t * 128:mt * S_l0attn + (t + 1) * 128],
                                    sb_id[:]), r=[bA_kf, b_id], w=[bpt])
                    p.op(VECTOR, tcopy(A_kntok[:, t * DH_l0attn + mt * 128: t * DH_l0attn + (mt + 1) * 128],
                                       pt[0:128, :]), r=[bpt], w=[bA_kntok])
                    pt, bpt = psT2t[1], bpsT2[1]
                    p.op(TENSOR, tp(pt[0:128, :], A_vf[:, mt * S_l0attn + t * 128:mt * S_l0attn + (t + 1) * 128],
                                    sb_id[:]), r=[bA_vf, b_id], w=[bpt])
                    p.op(SCALAR, act(A_vbtok[:, t * DH_l0attn + mt * 128: t * DH_l0attn + (mt + 1) * 128],
                                     pt[0:128, :], AF.Identity), r=[bpt], w=[bA_vbtok])
                p.op(VECTOR, ts(A_vbtok[:, t * DH_l0attn:(t + 1) * DH_l0attn],
                                A_vbtok[:, t * DH_l0attn:(t + 1) * DH_l0attn],
                                btok[:, t * 66 + bcol: t * 66 + bcol + 1],
                                None, ALU.mult), r=[b_btok], w=[bA_vbtok])
                p.op(VECTOR, ts(A_kbtok[:, t * DH_l0attn:(t + 1) * DH_l0attn],
                                A_kntok[:, t * DH_l0attn:(t + 1) * DH_l0attn],
                                btok[:, t * 66 + bcol: t * 66 + bcol + 1],
                                None, ALU.mult), r=[bA_kntok, b_btok], w=[bA_kbtok])

            if stage < 6:
                continue
            SLOTW = 15 * 128

            def ut_chunk(i, slot):
                if slot < 2:
                    arena, barena, base = A_raw, bA_raw, slot * SLOTW
                else:
                    arena, barena, base = A_raw2, bA_raw2, (slot - 2) * SLOTW
                Ls = arena[:, base:base + 128]
                Ns = arena[:, base + 128:base + 256]
                lads = arena[:, base + 256:base + 256 + 11 * 128]
                Qs = arena[:, base + 256 + 11 * 128:base + 256 + 13 * 128]
                pidx = (2 + slot) % 4
                psk = psP[pidx][:, 0:128]
                bpsk = bpsP[pidx]
                psw = psP[pidx][:, 384:512]
                psu0 = psP[pidx][:, 128:384]
                for mt in range(2):
                    p.op(TENSOR, mm(psk, A_kf[:, mt * S_l0attn + i * 128: mt * S_l0attn + (i + 1) * 128],
                                    A_kf[:, mt * S_l0attn + i * 128: mt * S_l0attn + (i + 1) * 128],
                                    start=(mt == 0), stop=(mt == 1)), r=[bA_kf], w=[bpsk])
                yield
                p.op(VECTOR, stt(Ls, psk, btok[:, i * 66 + bcol: i * 66 + bcol + 1],
                                 sb_sl[:], ALU.mult, ALU.mult),
                     r=[bpsk, b_btok, b_sl], w=[barena])
                yield
                pt, bpt = psT2t[slot % 2], bpsT2[slot % 2]
                p.op(TENSOR, tp(pt[0:128, :], Ls, sb_id[:]), r=[barena, b_id], w=[bpt])
                p.op(VECTOR, tcopy(Ns, pt[0:128, :]), r=[bpt], w=[barena])
                yield
                prevL, prevN = Ls, Ns
                for li, kk in enumerate((2, 4, 8, 16, 32, 64)):
                    Ld = lads[:, li * 128:(li + 1) * 128]
                    p.op(TENSOR, mm(psk, prevN, prevL, start=True, stop=True),
                         r=[barena], w=[bpsk])
                    yield
                    p.op(VECTOR, tcopy(Ld, psk), r=[bpsk], w=[barena])
                    yield
                    if kk != 64:
                        Nd = lads[:, (5 + li) * 128:(6 + li) * 128] if False else lads[:, (6 + li) * 128 - 128 + 128:(6 + li) * 128 + 0]
                        Nd = lads[:, (5 + li + 1) * 128:(5 + li + 2) * 128]
                        p.op(TENSOR, mm(psk, prevL, prevN, start=True, stop=True),
                             r=[barena], w=[bpsk])
                        yield
                        p.op(SCALAR, act(Nd, psk, AF.Identity), r=[bpsk], w=[barena])
                        yield
                        prevN = Nd
                    prevL = Ld
                p.op(VECTOR, tt(Qs[:, 0:128], Ns, sb_id[:], ALU.add), r=[barena, b_id],
                     w=[barena])
                yield
                qcur = 0
                for li in range(6):
                    Ld = lads[:, li * 128:(li + 1) * 128]
                    p.op(TENSOR, mm(psk, Ld, Qs[:, qcur * 128:(qcur + 1) * 128],
                                    start=True, stop=True), r=[barena], w=[bpsk])
                    yield
                    p.op(VECTOR, tt(Qs[:, (1 - qcur) * 128:(2 - qcur) * 128], psk,
                                    Qs[:, qcur * 128:(qcur + 1) * 128], ALU.add),
                         r=[bpsk, barena], w=[barena])
                    yield
                    qcur = 1 - qcur
                QTd = A_oall[:, (2 * h + 1) * S_l0attn + i * 128:(2 * h + 1) * S_l0attn + (i + 1) * 128]
                p.op(SCALAR, act(QTd, Qs[:, qcur * 128:(qcur + 1) * 128], AF.Identity),
                     r=[barena], w=[bA_oall])
                yield
                for mt in range(2):
                    pw = psw if mt == 1 else psk
                    p.op(TENSOR, mm(pw, A_kbtok[:, i * DH_l0attn + mt * 128: i * DH_l0attn + (mt + 1) * 128],
                                    QTd, start=True, stop=True),
                         r=[bA_kbtok, bA_oall], w=[bpsk])
                    yield
                for mt in range(2):
                    pw = psw if mt == 1 else psk
                    p.op(VECTOR, ts(A_kbtok[:, i * DH_l0attn + mt * 128: i * DH_l0attn + (mt + 1) * 128],
                                    pw, -1.0), r=[bpsk], w=[bA_kbtok])
                    yield
                p.op(TENSOR, mm(psu0, QTd, A_vbtok[:, i * DH_l0attn:(i + 1) * DH_l0attn],
                                start=True, stop=True), r=[bA_oall, bA_vbtok], w=[bpsk])
                yield
                p.op(SCALAR, act(A_vbtok[:, i * DH_l0attn:(i + 1) * DH_l0attn], psu0, AF.Identity),
                     r=[bpsk], w=[bA_vbtok])
                yield

            for ig in range(0, NTS_l0attn, 4):
                gens = [ut_chunk(ig + k, k) for k in range(4)]
                live = [True] * 4
                while any(live):
                    for gi in range(4):
                        if live[gi]:
                            try:
                                next(gens[gi])
                            except StopIteration:
                                live[gi] = False

            for i in range(NTS_l0attn):
                QTd = A_oall[:, (2 * h + 1) * S_l0attn + i * 128:(2 * h + 1) * S_l0attn + (i + 1) * 128]
                psk = psP[2][:, 0:128]
                for mt in range(2):
                    p.op(TENSOR, mm(psk, A_kf[:, mt * S_l0attn + i * 128: mt * S_l0attn + (i + 1) * 128],
                                    A_qf[:, mt * S_l0attn + i * 128: mt * S_l0attn + (i + 1) * 128],
                                    start=(mt == 0), stop=(mt == 1)),
                         r=[bA_kf, bA_qf], w=[bpsP[2]])
                p.op(VECTOR, tt(aT_sb, psk, sb_su[:], ALU.mult), r=[bpsP[2], b_su], w=[baT])
                if i > 0:
                    for mt in range(2):
                        p.op(TENSOR, mm(psU, A_kbtok[:, i * DH_l0attn + mt * 128: i * DH_l0attn + (mt + 1) * 128],
                                        S_sb[:, mt * 256:(mt + 1) * 256],
                                        start=(mt == 0), stop=(mt == 1)),
                             r=[bA_kbtok, bS], w=[bpsP[3]])
                    p.op(VECTOR, tt(unew, A_vbtok[:, i * DH_l0attn:(i + 1) * DH_l0attn], psU, ALU.add),
                         r=[bA_vbtok, bpsP[3]], w=[bunew])
                else:
                    p.op(VECTOR, tcopy(unew, A_vbtok[:, i * DH_l0attn:(i + 1) * DH_l0attn]),
                         r=[bA_vbtok], w=[bunew])
                for dvs in range(2):
                    pso = psP[2][:, 128 + dvs * 128:256 + dvs * 128]
                    if i > 0:
                        for mt in range(2):
                            p.op(TENSOR, mm(pso,
                                            S_sb[:, mt * 256 + dvs * 128: mt * 256 + (dvs + 1) * 128],
                                            A_qf[:, mt * S_l0attn + i * 128: mt * S_l0attn + (i + 1) * 128],
                                            start=(mt == 0), stop=False),
                                 r=[bS, bA_qf], w=[bpsP[2]])
                    p.op(TENSOR, mm(pso, unew[:, dvs * 128:(dvs + 1) * 128], aT_sb,
                                    start=(i == 0), stop=True),
                         r=[bunew, baT], w=[bpsP[2]])
                p.op(VECTOR, tcopy(
                    A_oall[:, (2 * h) * S_l0attn + i * 128: (2 * h) * S_l0attn + (i + 1) * 128],
                    psP[2][:, 128:256]), r=[bpsP[2]], w=[bA_oall])
                p.op(SCALAR, act(
                    A_oall[:, (2 * h + 1) * S_l0attn + i * 128: (2 * h + 1) * S_l0attn + (i + 1) * 128],
                    psP[2][:, 256:384], AF.Identity), r=[bpsP[2]], w=[bA_oall])
                for mt in range(2):
                    p.op(TENSOR, mm(psS[:, mt * 256:(mt + 1) * 256],
                                    A_kntok[:, i * DH_l0attn + mt * 128: i * DH_l0attn + (mt + 1) * 128],
                                    unew, start=(mt == 0), stop=(mt == 1)),
                         r=[bA_kntok, bunew], w=[bpsS])
                if i == 0:
                    p.op(VECTOR, tcopy(S_sb[:, 0:512], psS[:, 0:512]), r=[bpsS], w=[bS])
                else:
                    p.op(VECTOR, tt(S_sb[:, 0:512], S_sb[:, 0:512], psS[:, 0:512],
                                    ALU.add), r=[bpsS, bS], w=[bS])

            if stage < 61:
                continue
            for ch in range(NCH_l0attn):
                pspick, bps = (psA, bpsA) if ch % 2 == 0 else (psB, bpsB)
                grow = (gsp[32 * h:32 * h + 1, ch * 512:(ch + 1) * 512] if h < 3
                        else gspB[0:1, ch * 512:(ch + 1) * 512])
                p.op(TENSOR, mm(pspick, sb_ones4[32 * h:32 * h + 1, :] if h < 3
                                else sb_ones4[0:1, :], grow,
                                start=True, stop=True),
                     r=[b_ones4, b_gsp if h < 3 else b_gspB], w=[bps])
                p.op(VECTOR, tcopy(A_gbc[:, ch * 512:(ch + 1) * 512], pspick),
                     r=[bps], w=[bA_gbc])
                p.op(VECTOR, ts(A_vbtok[:, ch * 512:(ch + 1) * 512], pspick, -1.0, 1.0,
                                ALU.mult, ALU.add), r=[bps], w=[bA_vbtok])
            for mt in range(2 if stage >= 62 else 0):
                p.op(VECTOR, tt(A_d1[:, mt * S_l0attn:(mt + 1) * S_l0attn], A_vf[:, mt * S_l0attn:(mt + 1) * S_l0attn],
                                A_vbtok[:, 0:S_l0attn], ALU.mult), r=[bA_vf, bA_vbtok], w=[bA_d1])
                if stage >= 63:
                    p.op(VECTOR, scan(A_ema[:, mt * S_l0attn:(mt + 1) * S_l0attn], A_gbc[:, 0:S_l0attn],
                                      A_d1[:, mt * S_l0attn:(mt + 1) * S_l0attn], 0.0),
                         r=[bA_gbc, bA_d1], w=[bA_ema])
            if b == 0 and h == 0:
                dump("ema", A_ema[:, 0:256], [bA_ema])
                dump("ema_l", A_ema[:, 1024:1280], [bA_ema])
            if stage < 70:
                continue
            for ch in range(NCH_l0attn):
                pspick, bps = (psA, bpsA) if ch % 2 == 0 else (psB, bpsB)
                mrow = (msp[32 * h:32 * h + 1, ch * 512:(ch + 1) * 512] if h < 3
                        else mspB[0:1, ch * 512:(ch + 1) * 512])
                p.op(TENSOR, mm(pspick, sb_ones4[32 * h:32 * h + 1, :] if h < 3
                                else sb_ones4[0:1, :], mrow,
                                start=True, stop=True),
                     r=[b_ones4, b_msp if h < 3 else b_mspB], w=[bps])
                p.op(VECTOR, tcopy(A_mbc[:, ch * 512:(ch + 1) * 512], pspick),
                     r=[bps], w=[bA_mbc])
            for mt in range(2):
                oc = (2 * h + mt) * S_l0attn
                p.op(VECTOR, tt(A_d1[:, mt * S_l0attn:(mt + 1) * S_l0attn], A_ema[:, mt * S_l0attn:(mt + 1) * S_l0attn],
                                A_oall[:, oc:oc + S_l0attn], ALU.subtract),
                     r=[bA_ema, bA_oall], w=[bA_d1])
                p.op(VECTOR, tt(A_d1[:, mt * S_l0attn:(mt + 1) * S_l0attn], A_d1[:, mt * S_l0attn:(mt + 1) * S_l0attn],
                                A_mbc[:, 0:S_l0attn], ALU.mult), r=[bA_d1, bA_mbc], w=[bA_d1])
                p.op(VECTOR, tt(A_oall[:, oc:oc + S_l0attn], A_oall[:, oc:oc + S_l0attn],
                                A_d1[:, mt * S_l0attn:(mt + 1) * S_l0attn], ALU.add),
                     r=[bA_d1, bA_oall], w=[bA_oall])
            if b == 0 and h == 0:
                dump("omix", A_oall[:, 0:256], [bA_oall])
            for ch in range(NCH_l0attn):
                pspick, bps = (psA, bpsA) if ch % 2 == 0 else (psB, bpsB)
                for mt in range(2):
                    oc = (2 * h + mt) * S_l0attn
                    p.op(SCALAR, act(A_vbtok[:, ch * 512:(ch + 1) * 512],
                                     A_oall[:, oc + ch * 512: oc + (ch + 1) * 512],
                                     AF.Square), r=[bA_oall], w=[bA_vbtok])
                    p.op(TENSOR, mm(psS[0:1, 0:512], sb_onescol[:],
                                    A_vbtok[:, ch * 512:(ch + 1) * 512],
                                    start=(mt == 0), stop=(mt == 1)),
                         r=[bA_vbtok, b_onescol], w=[bpsS])
                p.op(SCALAR, act(rmsrow, psS[0:1, 0:512],
                                 AF.Ln, bias=EPSRMS[0:1, :],
                                 scale=consts[0:1, 6:7]), r=[bpsS, bconsts], w=[brmsrow])
                p.op(SCALAR, act(rmsrow, rmsrow, AF.Exp,
                                 scale=NHALF[0:1, :]), r=[brmsrow, bconsts], w=[brmsrow])
                p.op(TENSOR, mm(pspick, sb_onesrf[:], rmsrow,
                                start=True, stop=True), r=[b_onesrf, brmsrow], w=[bps])
                for mt in range(2):
                    oc = (2 * h + mt) * S_l0attn
                    p.op(VECTOR, stt(A_oall[:, oc + ch * 512: oc + (ch + 1) * 512],
                                     A_oall[:, oc + ch * 512: oc + (ch + 1) * 512],
                                     onw[:, mt:mt + 1], pspick, ALU.mult, ALU.mult),
                         r=[bps, b_onw, bA_oall], w=[bA_oall])

        if b == 0:
            dump("ofin", A_oall[:, 0:256], [bA_oall])
            dump("ofin_h1", A_oall[:, 2 * S_l0attn:2 * S_l0attn + 256], [bA_oall])
            dump("ofin_h3", A_oall[:, 6 * S_l0attn:6 * S_l0attn + 256], [bA_oall])
            dump("woT", A_wo[:, 0:256], [bA_wo])
        for t in range(NTS_l0attn if stage >= 90 else 0):
            pb = 2 * (t % 2)
            for kt in range(NKT_l0attn):
                for ch in range(2):
                    p.op(TENSOR, mm(psP[pb + ch][:, 0:512],
                                    A_oall[:, kt * S_l0attn + t * 128: kt * S_l0attn + (t + 1) * 128],
                                    A_wo[:, kt * 1024 + ch * 512: kt * 1024 + (ch + 1) * 512],
                                    start=(kt == 0), stop=(kt == NKT_l0attn - 1)),
                         r=[bA_oall, bA_wo], w=[bpsP[pb + ch]])
            p.op(VECTOR, tcopy(ln2o[:, 0:512], psP[pb][:, 0:512]), r=[bpsP[pb]], w=[bln2o])
            p.op(SCALAR, act(ln2o[:, 512:1024], psP[pb + 1][:, 0:512], AF.Identity),
                 r=[bpsP[pb + 1]], w=[bln2o])
            if b == 0 and t == 0:
                dump("attn_out", ln2o[:, 0:1024], [bln2o])
            ln_stats(ln2o[:], bln2o)
            xt = xt0
            bx = bxt[0]
            p.op(SYNC, dma(xt[:], x_in[b, t * 128:(t + 1) * 128, :]), w=[bx])
            p.op(VECTOR, ts(out_t[:], ln2o[:], stats[:, 2:3], stats[:, 7:8],
                            ALU.subtract, ALU.mult), r=[bln2o, bstats], w=[bout_t])
            p.op(VECTOR, tt(out_t[:], out_t[:], xt[:], ALU.add), r=[bout_t, bx], w=[bout_t])
            p.op(SYNC, dma(x_out[b, t * 128:(t + 1) * 128, :], out_t[:]),
                 r=[bout_t], sem="store")
            if out_xmax is not None:
                if b == 0 and t == 0:
                    p.op(VECTOR, memset(xnm[:, 0:1], 0.0), w=[bxnm])
                mu2, rstd2 = ln_stats(out_t[:], bout_t, sl=1)
                p.op(VECTOR, ts(xt[:], out_t[:], mu2, rstd2,
                                ALU.subtract, ALU.mult), r=[bout_t, bstats2[1]], w=[bx])
                p.op(VECTOR, red(xnm[:, 1:2], xt[:], op=ALU.max, absv=True),
                     r=[bx], w=[bxnm])
                p.op(VECTOR, tt(xnm[:, 0:1], xnm[:, 0:1], xnm[:, 1:2], ALU.max),
                     r=[bxnm], w=[bxnm])
    if out_xmax is not None:
        p.op(TENSOR, tp(psS[0:1, 0:128], xnm[:, 0:1], sb_idf[:]), r=[bxnm, b_idf], w=[bpsS])
        p.op(VECTOR, red(sxm[0:1, 0:1], psS[0:1, 0:128], op=ALU.max), r=[bpsS], w=[bsxm])
        p.op(VECTOR, tcopy(sxm[0:1, 1:16], sxm[0:1, 0:1].broadcast_to((1, 15))),
             r=[bsxm], w=[bsxm])
        p.op(SYNC, dma(out_xmax, sxm[0:1, 0:16]), r=[bsxm], sem="store")
    p.op(SYNC, lambda e: None, deps=[("store", p.cnt.get("store", 0)),
                                     ("dstore", p.cnt.get("dstore", 0))])


import numpy as np

D_l0ffn = 1024
DFF_l0ffn = 4096
NKT_l0ffn = D_l0ffn // 128


def build_l0_ffn(p: Prog, x_in, w, nseq=2, mode="full",
                 xmax_in=None, nxmax=1, hmax_in=None, nhmax=1,
                 x_out=None, out_xmax=None, out_hmax=None,
                 out_winsim=None, out_xsT=None, win_in=None, xsT_in=None,
                 out_hid=None, hid_in=None):
    nc = p.nc
    T = nseq * 2048
    NTS_l0ffn = T // 128
    NTC = T // 512
    TPB = 2048 // 128

    ident_bf = p.const(np.eye(128, dtype=BF16NP), "ident_bf")
    ident_f32 = p.const(np.eye(128, dtype=np.float32), "ident_f32")
    ones_row_f1 = p.const(np.ones((1, 128), np.float32), "ones_row_f1")
    cc = np.zeros((128, 2), np.float32)
    cc[:, 0] = 1e-5; cc[:, 1] = -0.5
    cvec = p.const(cc, "ln_consts")

    sb_id = p.sb("sb_id", [128, 128], BF16); b_id = Buf()
    sb_idf = p.sb("sb_idf", [128, 128], F32); b_idf = Buf()
    sb_onesr = p.sb("sb_onesr", [1, 128], F32); b_onesr = Buf()
    consts = p.sb("consts", [128, 2], F32); bconsts = Buf()
    EPS = consts[:, 0:1]; NHALF = consts[:, 1:2]
    p.op(SYNC, dma(sb_id[:], ident_bf), w=[b_id])
    p.op(SYNC, dma(sb_idf[:], ident_f32), w=[b_idf])
    p.op(SYNC, dma(sb_onesr[:], ones_row_f1), w=[b_onesr])
    p.op(SYNC, dma(consts[:], cvec), w=[bconsts])

    stats = p.sb("stats", [128, 16], F32); bstats = Buf()
    scr = p.sb("scr", [128, D_l0ffn], F32); bscr = Buf()
    srow = p.sb("srow", [1, 256], F32); bsrow = Buf()
    gsc = p.sb("gsc", [128, 8], F32); bgsc = Buf()
    xt0 = p.sb("xt0", [128, D_l0ffn], F32)
    xt1 = p.sb("xt1", [128, D_l0ffn], F32)
    bxt = [Buf(), Buf()]
    psP = [p.ps(f"psP{i}", [128, 512], F32) for i in range(4)]
    bpsP = [Buf() for _ in range(4)]
    psA, bpsA = psP[0][:, 0:512], bpsP[0]
    psB, bpsB = psP[1][:, 0:512], bpsP[1]
    psT2t = [p.ps(f"psT{i}", [128, 128], BF16) for i in range(2)]
    bpsT2 = [Buf(), Buf()]
    psT, bpsT = psT2t[0], bpsT2[0]
    psTf = psP[2][:, 0:128]; bpsTf = bpsP[2]
    psTf2 = psP[3][:, 0:128]; bpsTf2 = bpsP[3]

    def ln_stats(src_ap, bsrc):
        p.op(VECTOR, red(stats[:, 0:1], src_ap), r=[bsrc], w=[bstats])
        p.op(SCALAR, act(scr[:], src_ap, AF.Square, accum_out=stats[:, 1:2]),
             r=[bsrc], w=[bscr, bstats])
        p.op(VECTOR, ts(stats[:, 2:3], stats[:, 0:1], 1.0 / D_l0ffn), r=[bstats], w=[bstats])
        p.op(VECTOR, ts(stats[:, 3:4], stats[:, 2:3], -1.0), r=[bstats], w=[bstats])
        p.op(VECTOR, ts(stats[:, 4:5], stats[:, 1:2], 1.0 / D_l0ffn), r=[bstats], w=[bstats])
        p.op(VECTOR, stt(stats[:, 5:6], stats[:, 2:3], stats[:, 3:4], stats[:, 4:5],
                         ALU.mult, ALU.add), r=[bstats], w=[bstats])
        p.op(SCALAR, act(stats[:, 6:7], stats[:, 5:6], AF.Ln, bias=EPS),
             r=[bstats, bconsts], w=[bstats])
        p.op(SCALAR, act(stats[:, 7:8], stats[:, 6:7], AF.Exp, scale=NHALF),
             r=[bstats, bconsts], w=[bstats])

    def partition_max_to_out(colap, bcol, out_ap):
        p.op(TENSOR, tp(psTf[0:1, 0:128], colap, sb_idf[:]), r=[bcol, b_idf], w=[bpsTf])
        p.op(VECTOR, red(srow[0:1, 0:1], psTf[0:1, 0:128], op=ALU.max),
             r=[bpsTf], w=[bsrow])
        p.op(VECTOR, tcopy(srow[0:1, 1:16], srow[0:1, 0:1].broadcast_to((1, 15))),
             r=[bsrow], w=[bsrow])
        p.op(SYNC, dma(out_ap, srow[0:1, 0:16]), r=[bsrow], sem="store")

    def bcast_scalar(src11, bsrc, dstcol, bdst):
        p.op(TENSOR, mm(psTf[:, 0:1], sb_onesr[:], src11, start=True, stop=True),
             r=[b_onesr, bsrc], w=[bpsTf])
        p.op(VECTOR, tcopy(dstcol, psTf[:, 0:1]), r=[bpsTf], w=[bdst])

    if mode == "amax_x":
        xnm = p.sb("xnm", [128, 4], F32); bxnm = Buf()
        xn_t = p.sb("xn_t", [128, D_l0ffn], F32); bxn_t = Buf()
        p.op(VECTOR, memset(xnm[:, 0:1], 0.0), w=[bxnm])
        for t in range(NTS_l0ffn):
            b, ti = divmod(t, TPB)
            xt = xt0 if t % 2 == 0 else xt1
            bx = bxt[t % 2]
            p.op(SYNC, dma(xt[:], x_in[b, ti * 128:(ti + 1) * 128, :]), w=[bx])
            ln_stats(xt[:], bx)
            p.op(VECTOR, ts(xn_t[:], xt[:], stats[:, 2:3], stats[:, 7:8],
                            ALU.subtract, ALU.mult), r=[bx, bstats], w=[bxn_t])
            p.op(VECTOR, red(xnm[:, 1:2], xn_t[:], op=ALU.max, absv=True),
                 r=[bxn_t], w=[bxnm])
            p.op(VECTOR, tt(xnm[:, 0:1], xnm[:, 0:1], xnm[:, 1:2], ALU.max),
                 r=[bxnm], w=[bxnm])
        partition_max_to_out(xnm[:, 0:1], bxnm, out_xmax)
        p.op(SYNC, lambda e: None, deps=[("store", p.cnt.get("store", 0)),
                                     ("wspill", p.cnt.get("wspill", 0)),
                                     ("xspill", p.cnt.get("xspill", 0))])
        return

    p.op(GPSIMD, dma(srow[0:1, 32:32 + nxmax * 16],
                     xmax_in.rearrange("(one r) c -> one (r c)", one=1)), w=[bsrow])
    p.op(VECTOR, red(gsc[0:1, 0:1], srow[0:1, 32:32 + nxmax * 16], op=ALU.max),
         r=[bsrow], w=[bgsc])
    bcast_scalar(gsc[0:1, 0:1], bgsc, gsc[:, 0:1], bgsc)
    p.op(VECTOR, ts(gsc[:, 1:2], gsc[:, 0:1], 1e-8, None, ALU.max), r=[bgsc], w=[bgsc])
    p.op(VECTOR, recip(gsc[:, 2:3], gsc[:, 1:2]), r=[bgsc], w=[bgsc])
    p.op(VECTOR, ts(gsc[:, 2:3], gsc[:, 2:3], 32767.0), r=[bgsc], w=[bgsc])
    p.op(VECTOR, ts(gsc[:, 3:4], gsc[:, 1:2], 1.0 / 32767.0), r=[bgsc], w=[bgsc])

    if win_in is not None or hid_in is not None:
        win_hbm = win_in
        skip_win_quant = True
    else:
        win_hbm = out_winsim if out_winsim is not None else p.dram_tmp("win_sim", [D_l0ffn, DFF_l0ffn], BF16)
        skip_win_quant = False
    if skip_win_quant:
        pass
    camT = p.sb("camT", [128, 8], F32); bcam = Buf()
    amxT = p.sb("amxT", [128, 32], F32); bamx = Buf()
    sT = p.sb("sT", [128, 32], F32); bsT = Buf()
    rsT = p.sb("rsT", [128, 32], F32); brsT = Buf()
    sRow = p.sb("sRow", [1, DFF_l0ffn], F32); bsRow = Buf()
    rsRow = p.sb("rsRow", [1, DFF_l0ffn], F32); brsRow = Buf()
    sc = p.scratch()
    A_wf = sc.enter_context(nc.sbuf_tensor("A_wf", [128, DFF_l0ffn * 2], F32))
    s_bc = sc.enter_context(nc.sbuf_tensor("s_bc", [128, DFF_l0ffn], F32))
    rs_bc = sc.enter_context(nc.sbuf_tensor("rs_bc", [128, DFF_l0ffn], F32))
    itile = sc.enter_context(nc.sbuf_tensor("itile", [128, DFF_l0ffn], I32))
    wsimb = sc.enter_context(nc.sbuf_tensor("wsimb", [128, DFF_l0ffn], BF16))
    bwf = [Buf(), Buf()]; bs_bc = Buf(); brs_bc = Buf(); bitile = Buf(); bwsimb = Buf()

    for kt in range(NKT_l0ffn if not skip_win_quant else 0):
        wf = A_wf[:, (kt % 2) * DFF_l0ffn:(kt % 2 + 1) * DFF_l0ffn]
        bw = bwf[kt % 2]
        p.op(SYNC, dma(wf, w["W_in"][kt * 128:(kt + 1) * 128, :]), w=[bw])
        for c in range(32):
            ptf, bptf = (psTf, bpsTf) if c % 2 == 0 else (psTf2, bpsTf2)
            p.op(TENSOR, tp(ptf, wf[:, c * 128:(c + 1) * 128], sb_idf[:]),
                 r=[bw, b_idf], w=[bptf])
            p.op(VECTOR, red(camT[:, c % 2:c % 2 + 1], ptf, op=ALU.max, absv=True),
                 r=[bptf], w=[bcam])
            if kt == 0:
                p.op(VECTOR, tcopy(amxT[:, c:c + 1], camT[:, c % 2:c % 2 + 1]),
                     r=[bcam], w=[bamx])
            else:
                p.op(VECTOR, tt(amxT[:, c:c + 1], amxT[:, c:c + 1], camT[:, c % 2:c % 2 + 1],
                                ALU.max), r=[bcam, bamx], w=[bamx])
    if skip_win_quant:
        for _dummy in ():
            pass
    p.op(VECTOR, ts(amxT[:], amxT[:], 1e-8, None, ALU.max), r=[bamx], w=[bamx]) if not skip_win_quant else None
    if not skip_win_quant:
        p.op(VECTOR, recip(sT[:], amxT[:]), r=[bamx], w=[bsT])
        p.op(VECTOR, ts(sT[:], sT[:], 127.0), r=[bsT], w=[bsT])
        p.op(VECTOR, ts(rsT[:], amxT[:], 1.0 / 127.0), r=[bamx], w=[brsT])
    for c in range(32 if not skip_win_quant else 0):
        p.op(TENSOR, tp(psTf[0:1, 0:128], sT[:, c:c + 1], sb_idf[:]),
             r=[bsT, b_idf], w=[bpsTf])
        p.op(VECTOR, tcopy(sRow[0:1, c * 128:(c + 1) * 128], psTf[0:1, 0:128]),
             r=[bpsTf], w=[bsRow])
        p.op(TENSOR, tp(psTf[0:1, 0:128], rsT[:, c:c + 1], sb_idf[:]),
             r=[brsT, b_idf], w=[bpsTf])
        p.op(VECTOR, tcopy(rsRow[0:1, c * 128:(c + 1) * 128], psTf[0:1, 0:128]),
             r=[bpsTf], w=[brsRow])
    for ch in range(8):
        p.op(TENSOR, mm(psA[:], sb_onesr[:], sRow[0:1, ch * 512:(ch + 1) * 512],
                        start=True, stop=True), r=[b_onesr, bsRow], w=[bpsA])
        p.op(VECTOR, tcopy(s_bc[:, ch * 512:(ch + 1) * 512], psA[:]), r=[bpsA], w=[bs_bc])
        p.op(TENSOR, mm(psB[:], sb_onesr[:], rsRow[0:1, ch * 512:(ch + 1) * 512],
                        start=True, stop=True), r=[b_onesr, brsRow], w=[bpsB])
        p.op(VECTOR, tcopy(rs_bc[:, ch * 512:(ch + 1) * 512], psB[:]), r=[bpsB], w=[brs_bc])
    for kt in range(NKT_l0ffn if not skip_win_quant else 0):
        wf = A_wf[:, (kt % 2) * DFF_l0ffn:(kt % 2 + 1) * DFF_l0ffn]
        bw = bwf[kt % 2]
        p.op(SYNC, dma(wf, w["W_in"][kt * 128:(kt + 1) * 128, :]), w=[bw])
        p.op(VECTOR, tt(wf, wf, s_bc[:], ALU.mult), r=[bw, bs_bc], w=[bw])
        p.op(VECTOR, tcopy(itile[:], wf), r=[bw], w=[bitile])
        p.op(VECTOR, tt(wsimb[:], itile[:], rs_bc[:], ALU.mult),
             r=[bitile, brs_bc], w=[bwsimb])
        p.op(SYNC, dma(win_hbm[kt * 128:(kt + 1) * 128, :], wsimb[:]), r=[bwsimb],
             sem="wspill")
    sc.close()
    p.fence()

    bin_sb = p.sb("bin_sb", [128, 32], F32); bbin = Buf()
    p.op(GPSIMD, dma(bin_sb[:], w["b_in"].rearrange("(s p) -> p s", p=128),
                     allow_slow_non_contiguous=True), w=[bbin])

    if hid_in is None:
        A_wi = p.sb("A_wi", [128, 2 * NKT_l0ffn * 512], BF16)
        bwi = [Buf(), Buf()]

        def load_win_msg(msg):
            slot = msg % 2
            buf = A_wi[:, slot * NKT_l0ffn * 512:(slot + 1) * NKT_l0ffn * 512]
            p.op(SYNC, dma(buf.rearrange("p (kt sm) -> p kt sm", kt=NKT_l0ffn),
                           win_hbm[:, msg * 512:(msg + 1) * 512]
                           .rearrange("(kt p) sm -> p kt sm", p=128)), w=[bwi[slot]])
            return buf, bwi[slot]

        A_xs = p.sb("A_xs", [128, 2 * NKT_l0ffn * 512], BF16)
        bxs = [Buf(), Buf()]
        xq = p.sb("xq", [128, D_l0ffn], F32); bxq = Buf()
        ixq = p.sb("ixq", [128, D_l0ffn], I32); bixq = Buf()
        xsb = p.sb("xsb", [128, D_l0ffn], BF16); bxsb = Buf()
        hsc = p.sb("hsc", [128, 512], F32); bhsc = Buf()

    def xsim_chunk(tc, slot=0):
        off = slot * NKT_l0ffn * 512
        if xsT_in is not None:
            for kt in range(NKT_l0ffn):
                p.op(SYNC, dma(A_xs[:, off + kt * 512: off + (kt + 1) * 512],
                               xsT_in[kt * 128:(kt + 1) * 128, tc * 512:(tc + 1) * 512]),
                     w=[bxs[slot]])
            return
        for lt in range(4):
            t = tc * 4 + lt
            b, ti = divmod(t, TPB)
            xt = xt0 if t % 2 == 0 else xt1
            bx = bxt[t % 2]
            p.op(SYNC, dma(xt[:], x_in[b, ti * 128:(ti + 1) * 128, :]), w=[bx])
            ln_stats(xt[:], bx)
            p.op(VECTOR, ts(xq[:], xt[:], stats[:, 2:3], stats[:, 7:8],
                            ALU.subtract, ALU.mult), r=[bx, bstats], w=[bxq])
            p.op(VECTOR, ts(xq[:], xq[:], gsc[:, 2:3], None, ALU.mult),
                 r=[bxq, bgsc], w=[bxq])
            p.op(VECTOR, tcopy(ixq[:], xq[:]), r=[bxq], w=[bixq])
            p.op(VECTOR, ts(xsb[:], ixq[:], gsc[:, 3:4], None, ALU.mult),
                 r=[bixq, bgsc], w=[bxsb])
            for kt in range(NKT_l0ffn):
                pt, bpt = psT2t[kt % 2], bpsT2[kt % 2]
                p.op(TENSOR, tp(pt[0:128, :], xsb[:, kt * 128:(kt + 1) * 128], sb_id[:]),
                     r=[bxsb, b_id], w=[bpt])
                dst = A_xs[:, off + kt * 512 + lt * 128: off + kt * 512 + (lt + 1) * 128]
                if kt % 2 == 0:
                    p.op(VECTOR, tcopy(dst, pt[0:128, :]), r=[bpt], w=[bxs[slot]])
                else:
                    p.op(SCALAR, act(dst, pt[0:128, :], AF.Identity), r=[bpt], w=[bxs[slot]])
        if out_xsT is not None:
            for kt in range(NKT_l0ffn):
                p.op(SYNC, dma(out_xsT[kt * 128:(kt + 1) * 128, tc * 512:(tc + 1) * 512],
                               A_xs[:, off + kt * 512: off + (kt + 1) * 512]),
                     r=[bxs[slot]], sem="xspill")

    if mode == "amax_h":
        hm = p.sb("hm", [128, 4], F32); bhm = Buf()
        hbuf = p.sb("hbuf", [128, 8 * 512], BF16)
        bhbuf = [Buf() for _ in range(8)]
        p.op(VECTOR, memset(hm[:, 0:1], 0.0), w=[bhm])
        xsim_chunk(0, 0)
        for tc in range(NTC):
            slot = tc % 2
            off = slot * NKT_l0ffn * 512
            for msg in range(8):
                wbuf, bw = load_win_msg(msg)
                for kt in range(NKT_l0ffn):
                    for j in range(4):
                        p.op(TENSOR, mm(psP[j][:, 0:512],
                                        wbuf[:, kt * 512 + j * 128: kt * 512 + (j + 1) * 128],
                                        A_xs[:, off + kt * 512: off + (kt + 1) * 512],
                                        start=(kt == 0), stop=(kt == NKT_l0ffn - 1)),
                             r=[bw, bxs[slot]], w=[bpsP[j]])
                for j in range(4):
                    ms = msg * 4 + j
                    sl = ms % 8
                    hb = hbuf[:, sl * 512:(sl + 1) * 512]
                    bhb = bhbuf[sl]
                    p.op(SCALAR, act(hb, psP[j][:, 0:512], AF.Gelu,
                                     bias=bin_sb[:, ms:ms + 1]),
                         r=[bpsP[j], bbin], w=[bhb])
                    p.op(VECTOR, red(hm[:, 1:2], hb, op=ALU.max, absv=True),
                         r=[bhb], w=[bhm])
                    p.op(VECTOR, tt(hm[:, 0:1], hm[:, 0:1], hm[:, 1:2], ALU.max),
                         r=[bhm], w=[bhm])
                    if out_hid is not None:
                        p.op(SYNC, dma(out_hid[ms * 128:(ms + 1) * 128,
                                               tc * 512:(tc + 1) * 512], hb),
                             r=[bhb], sem=f"hspill{j}")
                if msg == 0 and tc + 1 < NTC:
                    xsim_chunk(tc + 1, (tc + 1) % 2)
        partition_max_to_out(hm[:, 0:1], bhm, out_hmax)
        p.op(SYNC, lambda e: None, deps=[("store", p.cnt.get("store", 0)),
                                         ("wspill", p.cnt.get("wspill", 0)),
                                         ("xspill", p.cnt.get("xspill", 0))]
                                        + [(f"hspill{j}", p.cnt.get(f"hspill{j}", 0))
                                           for j in range(4)])
        return

    p.op(GPSIMD, dma(srow[0:1, 64:64 + nhmax * 16],
                     hmax_in.rearrange("(one r) c -> one (r c)", one=1)), w=[bsrow])
    p.op(VECTOR, red(gsc[0:1, 4:5], srow[0:1, 64:64 + nhmax * 16], op=ALU.max),
         r=[bsrow], w=[bgsc])
    bcast_scalar(gsc[0:1, 4:5], bgsc, gsc[:, 4:5], bgsc)
    p.op(VECTOR, ts(gsc[:, 5:6], gsc[:, 4:5], 1e-8, None, ALU.max), r=[bgsc], w=[bgsc])
    p.op(VECTOR, recip(gsc[:, 6:7], gsc[:, 5:6]), r=[bgsc], w=[bgsc])
    p.op(VECTOR, ts(gsc[:, 6:7], gsc[:, 6:7], 32767.0), r=[bgsc], w=[bgsc])
    p.op(VECTOR, ts(gsc[:, 7:8], gsc[:, 5:6], 1.0 / 32767.0), r=[bgsc], w=[bgsc])

    A_wout = p.sb("A_wout", [128, 32 * D_l0ffn], BF16); bwout = Buf()
    wst = p.sb("wst", [128, 4], F32); bwst = Buf()
    sc2 = p.scratch()
    wrow = sc2.enter_context(nc.sbuf_tensor("wrow", [128, 2 * D_l0ffn], F32))
    irow = sc2.enter_context(nc.sbuf_tensor("irow", [128, D_l0ffn], I32))
    bwrow = [Buf(), Buf()]; birow = Buf()
    for kt in range(32):
        wf = wrow[:, (kt % 2) * D_l0ffn:(kt % 2 + 1) * D_l0ffn]
        bw = bwrow[kt % 2]
        p.op(SYNC, dma(wf, w["W_out"][kt * 128:(kt + 1) * 128, :]), w=[bw])
        p.op(VECTOR, red(wst[:, 0:1], wf, op=ALU.max, absv=True), r=[bw], w=[bwst])
        p.op(VECTOR, ts(wst[:, 0:1], wst[:, 0:1], 1e-8, None, ALU.max), r=[bwst], w=[bwst])
        p.op(VECTOR, recip(wst[:, 1:2], wst[:, 0:1]), r=[bwst], w=[bwst])
        p.op(VECTOR, ts(wst[:, 1:2], wst[:, 1:2], 127.0), r=[bwst], w=[bwst])
        p.op(VECTOR, ts(wst[:, 2:3], wst[:, 0:1], 1.0 / 127.0), r=[bwst], w=[bwst])
        p.op(VECTOR, ts(wf, wf, wst[:, 1:2], None, ALU.mult), r=[bw, bwst], w=[bw])
        p.op(VECTOR, tcopy(irow[:], wf), r=[bw], w=[birow])
        p.op(VECTOR, ts(A_wout[:, kt * D_l0ffn:(kt + 1) * D_l0ffn], irow[:], wst[:, 2:3],
                        None, ALU.mult), r=[birow, bwst], w=[bwout])
    sc2.close()
    p.fence()

    bout_row = p.sb("bout_row", [1, D_l0ffn], F32); bbout = Buf()
    p.op(SYNC, dma(bout_row[:], w["b_out"].rearrange("(one m) -> one m", one=1)),
         w=[bbout])

    A_h = p.sb("A_h", [128, 32 * 512], BF16); bh = Buf()
    hq = p.sb("hq", [128, 512], F32); bhq = Buf()
    ih = p.sb("ih", [128, 512], I32); bih = Buf()
    ln2o = p.sb("ln2o", [128, D_l0ffn], F32); bln2o = Buf()
    out_t = p.sb("out_t", [128, D_l0ffn], F32); bout_t = Buf()

    if hid_in is not None:
        A_hload = p.sb("A_hload", [128, 4 * 8 * 512], BF16)
        bhload = [Buf() for _ in range(4)]

    for tc in range(NTC):
        if hid_in is not None:
            for g in range(4):
                dst = A_hload[:, g * 4096:(g + 1) * 4096]
                p.op(SYNC, dma(dst.rearrange("p (kt c) -> p kt c", kt=8),
                               hid_in[g * 1024:(g + 1) * 1024, tc * 512:(tc + 1) * 512]
                               .rearrange("(kt p) c -> p kt c", p=128)),
                     w=[bhload[g]])
            for ms in range(32):
                g = ms // 8
                src = A_hload[:, ms * 512:(ms + 1) * 512]
                p.op(SCALAR, act(hq[:], src, AF.Identity, scale=gsc[:, 6:7]),
                     r=[bhload[g], bgsc], w=[bhq])
                p.op(VECTOR, tcopy(ih[:], hq[:]), r=[bhq], w=[bih])
                p.op(VECTOR, ts(A_h[:, ms * 512:(ms + 1) * 512], ih[:], gsc[:, 7:8],
                                None, ALU.mult), r=[bih, bgsc], w=[bh])
        else:
            xsim_chunk(tc)
            for msg in range(8):
                wbuf, bw = load_win_msg(msg)
                for kt in range(NKT_l0ffn):
                    for j in range(4):
                        p.op(TENSOR, mm(psP[j][:, 0:512],
                                        wbuf[:, kt * 512 + j * 128: kt * 512 + (j + 1) * 128],
                                        A_xs[:, kt * 512:(kt + 1) * 512],
                                        start=(kt == 0), stop=(kt == NKT_l0ffn - 1)),
                             r=[bw, bxs[0]], w=[bpsP[j]])
                for j in range(4):
                    ms = msg * 4 + j
                    p.op(SCALAR, act(hsc[:], psP[j][:, 0:512], AF.Gelu, bias=bin_sb[:, ms:ms + 1]),
                         r=[bpsP[j], bbin], w=[bhsc])
                    p.op(SCALAR, act(hq[:], hsc[:], AF.Identity, scale=gsc[:, 6:7]),
                         r=[bhsc, bgsc], w=[bhq])
                    p.op(VECTOR, tcopy(ih[:], hq[:]), r=[bhq], w=[bih])
                    p.op(VECTOR, ts(A_h[:, ms * 512:(ms + 1) * 512], ih[:], gsc[:, 7:8],
                                    None, ALU.mult), r=[bih, bgsc], w=[bh])
        for tm in range(4):
            pb = 2 * (tm % 2)
            for kt in range(32):
                for ch in range(2):
                    p.op(TENSOR, mm(psP[pb + ch][:, 0:512],
                                    A_h[:, kt * 512 + tm * 128: kt * 512 + (tm + 1) * 128],
                                    A_wout[:, kt * D_l0ffn + ch * 512: kt * D_l0ffn + (ch + 1) * 512],
                                    start=(kt == 0), stop=False),
                         r=[bh, bwout], w=[bpsP[pb + ch]])
            for ch in range(2):
                p.op(TENSOR, mm(psP[pb + ch][:, 0:512], sb_onesr[:],
                                bout_row[0:1, ch * 512:(ch + 1) * 512],
                                start=False, stop=True), r=[b_onesr, bbout],
                     w=[bpsP[pb + ch]])
            p.op(VECTOR, tcopy(ln2o[:, 0:512], psP[pb][:, 0:512]), r=[bpsP[pb]], w=[bln2o])
            p.op(SCALAR, act(ln2o[:, 512:1024], psP[pb + 1][:, 0:512], AF.Identity),
                 r=[bpsP[pb + 1]], w=[bln2o])
            ln_stats(ln2o[:], bln2o)
            t = tc * 4 + tm
            b, ti = divmod(t, TPB)
            xt = xt0 if t % 2 == 0 else xt1
            bx = bxt[t % 2]
            p.op(SYNC, dma(xt[:], x_in[b, ti * 128:(ti + 1) * 128, :]), w=[bx])
            p.op(VECTOR, ts(out_t[:], ln2o[:], stats[:, 2:3], stats[:, 7:8],
                            ALU.subtract, ALU.mult), r=[bln2o, bstats], w=[bout_t])
            p.op(VECTOR, tt(out_t[:], out_t[:], xt[:], ALU.add), r=[bout_t, bx], w=[bout_t])
            p.op(SYNC, dma(x_out[b, ti * 128:(ti + 1) * 128, :], out_t[:]),
                 r=[bout_t], sem="store")
    p.op(SYNC, lambda e: None, deps=[("store", p.cnt.get("store", 0))])


import numpy as np

D_l1attn = 1024
S_l1attn = 2048
H_l1attn = 16
HD_l1attn = 64
NKT_l1attn = D_l1attn // 128
NTS_l1attn = S_l1attn // 128
NCH_l1attn = S_l1attn // 512
NEGBIG_l1attn = -30000.0


def build_l1_attn(p: Prog, x_in, x_out, w, nseq=2):
    nc = p.nc

    ident_bf = p.const(np.eye(128, dtype=BF16NP), "ident_bf")
    negbig_ident = p.const((NEGBIG_l1attn * np.eye(128)).astype(BF16NP), "negbig_ident")
    ones_row_bf = p.const(np.ones((1, 128), BF16NP), "ones_row_bf")
    ones_row_f32 = p.const(np.ones((1, 64), np.float32), "ones_row_f32")
    bandm = np.zeros((128, 4 * 512), np.float32)
    pp = np.arange(128)[:, None]
    for j in range(4):
        ff = np.arange(512)[None, :]
        bandm[:, j * 512:(j + 1) * 512] = (ff < (128 * j + pp))
    bands = p.const(bandm.astype(BF16NP), "bands")
    cc = np.zeros((128, 4), np.float32)
    cc[:, 0] = 1e-5; cc[:, 1] = -0.5; cc[:, 2] = 0.125; cc[:, 3] = -1.0
    cvec = p.const(cc, "ln_consts_l1a")

    A_xn = p.sb("A_xn", [128, NKT_l1attn * S_l1attn], BF16)
    A_q = p.sb("A_q", [128, NKT_l1attn * S_l1attn], BF16)
    A_k = p.sb("A_k", [128, NKT_l1attn * S_l1attn], BF16)
    A_v = p.sb("A_v", [128, 16 * 65 * NTS_l1attn // (S_l1attn // 128) * (S_l1attn // 128)], BF16)
    A_w = p.sb("A_w", [128, 8192], BF16)
    A_pt = p.sb("A_pt", [128, 512 * 8], BF16)
    A_bc = p.sb("A_bc", [128, 512], F32)
    A_rl0 = p.sb("A_rl0", [1, 512], F32)
    A_rl1 = p.sb("A_rl1", [1, 512], F32)
    xt0 = p.sb("xt0", [128, D_l1attn], F32)
    xt1 = p.sb("xt1", [128, D_l1attn], F32)
    xn_tok0 = p.sb("xn_tok0", [128, D_l1attn], BF16)
    xn_tok1 = p.sb("xn_tok1", [128, D_l1attn], BF16)
    scr = p.sb("scr", [128, D_l1attn], F32)
    out_t0 = p.sb("out_t0", [128, D_l1attn], F32)
    out_t1 = p.sb("out_t1", [128, D_l1attn], F32)
    stats = p.sb("stats", [128, 16], F32)
    consts = p.sb("consts", [128, 4], F32)
    bias_qk = p.sb("bias_qk", [128, 16], F32)
    bias_v = p.sb("bias_v", [1, 1024], BF16)
    bias_pr = p.sb("bias_pr", [1, 1024], BF16)
    ln2o0 = p.sb("ln2o0", [128, D_l1attn], F32)
    ln2o1 = p.sb("ln2o1", [128, D_l1attn], F32)
    sb_id = p.sb("sb_id", [128, 128], BF16)
    sb_nid = p.sb("sb_nid", [128, 128], BF16)
    sb_ones = p.sb("sb_ones", [1, 128], BF16)
    sb_ones_f = p.sb("sb_ones_f", [1, 64], F32)
    sb_band = p.sb("sb_band", [128, 512 * 4], BF16)

    psP = [p.ps(f"psP{i}", [128, 512], F32) for i in range(4)]
    bpsP = [Buf() for _ in range(4)]
    psOs = [p.ps(f"psO{i}", [128, 512], F32) for i in range(2)]
    bpsOs = [Buf(), Buf()]
    psT2t = [p.ps(f"psT{i}", [128, 128], BF16) for i in range(2)]
    bpsT2 = [Buf(), Buf()]
    psBC = psP[3]
    bpsBC = bpsP[3]

    bA_xn = Buf(); bA_q = Buf(); bA_k = Buf(); bA_v = Buf(); bA_w = Buf()
    bA_pt = [Buf() for _ in range(8)]
    bA_bc = Buf(); bA_rl0 = Buf(); bA_rl1 = Buf()
    bxt = [Buf(), Buf()]
    bxnt = [Buf(), Buf()]
    bscr = Buf(); bstats2 = [Buf(), Buf()]; bconsts = Buf()
    bout = [Buf(), Buf()]
    bbias_qk = Buf(); bbias_v = Buf(); bbias_pr = Buf()
    bln2o = [Buf(), Buf()]
    b_id = Buf(); b_nid = Buf(); b_ones = Buf(); b_band = Buf()

    EPS = consts[:, 0:1]; NHALF = consts[:, 1:2]; INV8 = consts[:, 2:3]
    NEG1 = consts[:, 3:4]; NEG1R = consts[0:1, 3:4]

    p.op(SYNC, dma(consts[:], cvec), w=[bconsts])
    p.op(GPSIMD, dma(bias_qk[:, 0:16],
                     w["c_attn_b"][0:2048].rearrange("(s p) -> p s", p=128),
                     allow_slow_non_contiguous=True),
         w=[bbias_qk])
    p.op(GPSIMD, dma(bias_v[:], w["c_attn_b"][2048:3072].rearrange("(one m) -> one m", one=1)),
         w=[bbias_v])
    p.op(GPSIMD, dma(bias_pr[:], w["c_proj_b"].rearrange("(one m) -> one m", one=1)), w=[bbias_pr])
    p.op(SYNC, dma(sb_id[:], ident_bf), w=[b_id])
    p.op(SYNC, dma(sb_nid[:], negbig_ident), w=[b_nid])
    p.op(SYNC, dma(sb_ones[:], ones_row_bf), w=[b_ones])
    p.op(SYNC, dma(sb_ones_f[:], ones_row_f32), w=[b_ones])
    p.op(SYNC, dma(sb_band[:], bands), w=[b_band])

    def ln_stats(src_ap, bsrc, sl=0):
        c = 8 * sl
        bst = bstats2[sl]
        p.op(VECTOR, red(stats[:, c + 0:c + 1], src_ap), r=[bsrc], w=[bst])
        p.op(SCALAR, act(scr[:], src_ap, AF.Square, accum_out=stats[:, c + 1:c + 2]),
             r=[bsrc], w=[bst])
        p.op(VECTOR, ts(stats[:, c + 2:c + 3], stats[:, c + 0:c + 1], 1.0 / D_l1attn),
             r=[bst], w=[bst])
        p.op(VECTOR, ts(stats[:, c + 3:c + 4], stats[:, c + 2:c + 3], -1.0),
             r=[bst], w=[bst])
        p.op(VECTOR, ts(stats[:, c + 4:c + 5], stats[:, c + 1:c + 2], 1.0 / D_l1attn),
             r=[bst], w=[bst])
        p.op(VECTOR, stt(stats[:, c + 5:c + 6], stats[:, c + 2:c + 3], stats[:, c + 3:c + 4],
                         stats[:, c + 4:c + 5], ALU.mult, ALU.add), r=[bst], w=[bst])
        p.op(SCALAR, act(stats[:, c + 6:c + 7], stats[:, c + 5:c + 6], AF.Ln, bias=EPS),
             r=[bst, bconsts], w=[bst])
        p.op(SCALAR, act(stats[:, c + 7:c + 8], stats[:, c + 6:c + 7], AF.Exp, scale=NHALF),
             r=[bst, bconsts], w=[bst])
        return stats[:, c + 2:c + 3], stats[:, c + 7:c + 8]

    for b in range(nseq):
        for t in range(NTS_l1attn):
            xt = xt0 if t % 2 == 0 else xt1
            bx = bxt[t % 2]
            p.op(SYNC, dma(xt[:], x_in[b, t * 128:(t + 1) * 128, :]), w=[bx])
            mu, rstd = ln_stats(xt[:], bx, sl=t % 2)
            xnt = xn_tok0 if t % 2 == 0 else xn_tok1
            bxn = bxnt[t % 2]
            p.op(VECTOR, ts(xnt[:], xt[:], mu, rstd,
                            ALU.subtract, ALU.mult), r=[bx, bstats2[t % 2]], w=[bxn])
            for kt in range(NKT_l1attn):
                pt, bpt = psT2t[kt % 2], bpsT2[kt % 2]
                p.op(TENSOR, tp(pt[:, 0:128], xnt[:, kt * 128:(kt + 1) * 128], sb_id[:]),
                     r=[bxn, b_id], w=[bpt])
                dst = A_xn[:, kt * S_l1attn + t * 128: kt * S_l1attn + (t + 1) * 128]
                if kt % 2 == 0:
                    p.op(VECTOR, tcopy(dst, pt[:, 0:128]), r=[bpt], w=[bA_xn])
                else:
                    p.op(SCALAR, act(dst, pt[:, 0:128], AF.Identity), r=[bpt], w=[bA_xn])

        for half in range(2):
            dst, bdst = (A_q, bA_q) if half == 0 else (A_k, bA_k)
            p.op(GPSIMD, dma(A_w[:, 0:8192].rearrange("p (kt m) -> p kt m", kt=8),
                             w["c_attn_w"][:, half * 1024:(half + 1) * 1024]
                             .rearrange("(kt p) m -> p kt m", p=128)),
                 w=[bA_w])
            for ms in range(8):
                for kt in range(NKT_l1attn):
                    for ch in range(NCH_l1attn):
                        p.op(TENSOR, mm(psP[ch][:, 0:512],
                                        A_w[:, kt * 1024 + ms * 128: kt * 1024 + (ms + 1) * 128],
                                        A_xn[:, kt * S_l1attn + ch * 512: kt * S_l1attn + (ch + 1) * 512],
                                        start=(kt == 0), stop=(kt == NKT_l1attn - 1)),
                             r=[bA_w, bA_xn], w=[bpsP[ch]])
                for ch in range(NCH_l1attn):
                    dd = dst[:, ms * S_l1attn + ch * 512: ms * S_l1attn + (ch + 1) * 512]
                    bcol = bias_qk[:, 8 * half + ms: 8 * half + ms + 1]
                    p.op(SCALAR, act(dd, psP[ch][:, 0:512], AF.Identity, bias=bcol),
                         r=[bpsP[ch], bbias_qk], w=[bdst])

        p.op(GPSIMD, dma(A_w[:, 0:8192].rearrange("p (kt m) -> p kt m", kt=8),
                         w["c_attn_w"][:, 2048:3072].rearrange("(kt p) m -> p kt m", p=128)),
             w=[bA_w])
        for t in range(NTS_l1attn):
            vbase = t * 1040
            pb = 2 * (t % 2)
            p.op(VECTOR, memset(A_v[:, vbase:vbase + 1040], 1.0), w=[bA_v])
            for kt in range(NKT_l1attn):
                for ch in range(2):
                    p.op(TENSOR, mm(psP[pb + ch][:, 0:512],
                                    A_xn[:, kt * S_l1attn + t * 128: kt * S_l1attn + (t + 1) * 128],
                                    A_w[:, kt * 1024 + ch * 512: kt * 1024 + (ch + 1) * 512],
                                    start=(kt == 0), stop=False),
                         r=[bA_w, bA_xn], w=[bpsP[pb + ch]])
            for ch in range(2):
                p.op(TENSOR, mm(psP[pb + ch][:, 0:512], sb_ones[:],
                                bias_v[:, ch * 512:(ch + 1) * 512],
                                start=False, stop=True),
                     r=[b_ones, bbias_v], w=[bpsP[pb + ch]])
                dst = A_v[:, vbase + ch * 8 * 65: vbase + (ch + 1) * 8 * 65]
                dst = dst.rearrange("p (h c) -> p h c", h=8)[:, :, 0:64]
                p.op(VECTOR, tcopy(dst, psP[pb + ch][:, 0:512].rearrange("p (h c) -> p h c", h=8)),
                     r=[bpsP[pb + ch]], w=[bA_v])

        for hp in range(H_l1attn // 2):
            qrow = hp * S_l1attn
            for qc in range(4):
                steps = [(ks, j) for ks in range(qc + 1) for j in range(4)]
                pend = []
                started = [False, False]
                for pi in range(0, len(steps), 2):
                    pair = steps[pi:pi + 2]
                    for si, (ks, j) in enumerate(pair):
                        diag = (ks == qc)
                        ktile = ks * 4 + j
                        for hh in range(2):
                            bank = 2 * si + hh
                            prow = 64 * hh
                            p.op(TENSOR, mm(psP[bank][:, 0:512],
                                            A_k[prow:prow + HD_l1attn,
                                                qrow + ktile * 128: qrow + (ktile + 1) * 128],
                                            A_q[prow:prow + HD_l1attn,
                                                qrow + qc * 512: qrow + (qc + 1) * 512],
                                            start=True, stop=not diag),
                                 r=[bA_k, bA_q], w=[bpsP[bank]])
                    for si, (ks, j) in enumerate(pair):
                        if ks == qc:
                            for hh in range(2):
                                bank = 2 * si + hh
                                p.op(TENSOR, mm(psP[bank][:, 0:512], sb_nid[:],
                                                sb_band[:, j * 512:(j + 1) * 512],
                                                start=False, stop=True),
                                     r=[b_nid, b_band], w=[bpsP[bank]])
                    for (slot, hh, vb, st) in pend:
                        p.op(TENSOR, mm(psOs[hh][0:65, :], A_v[:, vb:vb + 65],
                                        A_pt[:, slot * 512:(slot + 1) * 512],
                                        start=st, stop=False),
                             r=[bA_v, bA_pt[slot]], w=[bpsOs[hh]])
                    pend = []
                    for si, (ks, j) in enumerate(pair):
                        for hh in range(2):
                            bank = 2 * si + hh
                            slot = ((pi // 2) % 2) * 4 + bank
                            p.op(SCALAR, act(A_pt[:, slot * 512:(slot + 1) * 512],
                                             psP[bank][:, 0:512], AF.Exp, scale=INV8),
                                 r=[bpsP[bank], bconsts], w=[bA_pt[slot]])
                            h = 2 * hp + hh
                            vb = (ks * 4 + j) * 1040 + h * 65
                            pend.append((slot, hh, vb, not started[hh]))
                            started[hh] = True
                last = {}
                for i, (slot, hh, vb, st) in enumerate(pend):
                    last[hh] = i
                for i, (slot, hh, vb, st) in enumerate(pend):
                    p.op(TENSOR, mm(psOs[hh][0:65, :], A_v[:, vb:vb + 65],
                                    A_pt[:, slot * 512:(slot + 1) * 512],
                                    start=st, stop=(last[hh] == i)),
                         r=[bA_v, bA_pt[slot]], w=[bpsOs[hh]])
                p.op(VECTOR, recip(A_rl0[:], psOs[0][64:65, :]), r=[bpsOs[0]], w=[bA_rl0])
                p.op(VECTOR, recip(A_rl1[:], psOs[1][64:65, :]), r=[bpsOs[1]], w=[bA_rl1])
                p.op(TENSOR, mm(psBC[0:64, :], sb_ones_f[:], A_rl0[:], start=True, stop=True),
                     r=[b_ones, bA_rl0], w=[bpsBC])
                p.op(TENSOR, mm(psBC[64:128, :], sb_ones_f[:], A_rl1[:], start=True, stop=True),
                     r=[b_ones, bA_rl1], w=[bpsBC])
                p.op(VECTOR, tcopy(A_bc[:], psBC[:]), r=[bpsBC], w=[bA_bc])
                ofcol = hp * S_l1attn + qc * 512
                for hh in range(2):
                    prow = 64 * hh
                    p.op(VECTOR, tt(A_xn[prow:prow + HD_l1attn, ofcol: ofcol + 512],
                                    psOs[hh][0:64, :], A_bc[prow:prow + HD_l1attn, :], ALU.mult),
                         r=[bpsOs[hh], bA_bc], w=[bA_xn])

        p.op(GPSIMD, dma(A_w[:, 0:8192].rearrange("p (kt m) -> p kt m", kt=8),
                         w["c_proj_w"].rearrange("(kt p) m -> p kt m", p=128)),
             w=[bA_w])
        for t in range(NTS_l1attn):
            pb = 2 * (t % 2)
            sl = t % 2
            ln2o = ln2o0 if sl == 0 else ln2o1
            out_t = out_t0 if sl == 0 else out_t1
            for kt in range(NKT_l1attn):
                for ch in range(2):
                    p.op(TENSOR, mm(psP[pb + ch][:, 0:512],
                                    A_xn[:, kt * S_l1attn + t * 128:kt * S_l1attn + (t + 1) * 128],
                                    A_w[:, kt * 1024 + ch * 512: kt * 1024 + (ch + 1) * 512],
                                    start=(kt == 0), stop=False),
                         r=[bA_w, bA_xn], w=[bpsP[pb + ch]])
            for ch in range(2):
                p.op(TENSOR, mm(psP[pb + ch][:, 0:512], sb_ones[:],
                                bias_pr[:, ch * 512:(ch + 1) * 512],
                                start=False, stop=True),
                     r=[b_ones, bbias_pr], w=[bpsP[pb + ch]])
            p.op(VECTOR, tcopy(ln2o[:, 0:512], psP[pb][:, 0:512]), r=[bpsP[pb]], w=[bln2o[sl]])
            p.op(SCALAR, act(ln2o[:, 512:1024], psP[pb + 1][:, 0:512], AF.Identity),
                 r=[bpsP[pb + 1]], w=[bln2o[sl]])
            mu, rstd = ln_stats(ln2o[:], bln2o[sl], sl=sl)
            xt = xt0 if t % 2 == 0 else xt1
            bx = bxt[t % 2]
            p.op(SYNC, dma(xt[:], x_in[b, t * 128:(t + 1) * 128, :]), w=[bx])
            p.op(VECTOR, ts(out_t[:], ln2o[:], mu, rstd,
                            ALU.subtract, ALU.mult), r=[bln2o[sl], bstats2[sl]], w=[bout[sl]])
            p.op(VECTOR, tt(out_t[:], out_t[:], xt[:], ALU.add), r=[bout[sl], bx], w=[bout[sl]])
            p.op(SYNC, dma(x_out[b, t * 128:(t + 1) * 128, :], out_t[:]),
                 r=[bout[sl]], sem="store")

    p.op(SYNC, lambda e: None, deps=[("store", p.cnt.get("store", 0))])


import numpy as np

D_l1ffn = 1024
S_l1ffn = 2048
NKT_l1ffn = D_l1ffn // 128
NTS_l1ffn = S_l1ffn // 128
NCH_l1ffn = S_l1ffn // 512


def build_l1_ffn(p: Prog, x_in, x_out, w, nseq=2):
    nc = p.nc

    ident_bf = p.const(np.eye(128, dtype=BF16NP), "ident_bf")
    ident_f32 = p.const(np.eye(128, dtype=np.float32), "ident_f32")
    ones_row_f1 = p.const(np.ones((1, 128), np.float32), "ones_row_f1")
    ones_row16 = p.const(np.ones((1, 16), np.float32), "ones_row16")
    ones_row_bf = p.const(np.ones((1, 128), BF16NP), "ones_row_bf")
    ones_row16_bf = p.const(np.ones((1, 16), BF16NP), "ones_row16_bf")
    ones_col_bf = p.const(np.ones((128, 1), BF16NP), "ones_col_bf")
    cc = np.zeros((128, 3), np.float32)
    cc[:, 0] = 1e-5; cc[:, 1] = -0.5; cc[:, 2] = 1.0 / 32.0
    cvec = p.const(cc, "ln_consts3")
    rowval8 = p.const((2 * np.arange(8)[None, :].repeat(128, 0)
                       + (np.arange(128) // 64)[:, None]).astype(np.float32), "rowval8")
    rowidx16 = p.const(np.arange(16, dtype=np.float32).reshape(16, 1), "rowidx16")

    sb_id = p.sb("sb_id", [128, 128], BF16); b_id = Buf()
    sb_idf = p.sb("sb_idf", [128, 128], F32); b_idf = Buf()
    sb_onesr = p.sb("sb_onesr", [1, 128], F32); b_onesr = Buf()
    sb_ones16 = p.sb("sb_ones16", [1, 16], F32); b_ones16 = Buf()
    sb_onesb = p.sb("sb_onesb", [1, 128], BF16); b_onesb = Buf()
    sb_ones16b = p.sb("sb_ones16b", [1, 16], BF16); b_ones16b = Buf()
    sb_onescol = p.sb("sb_onescol", [128, 1], BF16); b_onescol = Buf()
    sb_rv8 = p.sb("sb_rv8", [128, 8], F32); b_rv8 = Buf()
    sb_ri16 = p.sb("sb_ri16", [16, 1], F32); b_ri16 = Buf()
    consts = p.sb("consts", [128, 3], F32); bconsts = Buf()
    EPS = consts[:, 0:1]; NHALF = consts[:, 1:2]; INV32 = consts[:, 2:3]
    for dst, src, bb in ((sb_id, ident_bf, b_id), (sb_idf, ident_f32, b_idf),
                         (sb_onesr, ones_row_f1, b_onesr), (sb_ones16, ones_row16, b_ones16),
                         (sb_onesb, ones_row_bf, b_onesb), (sb_ones16b, ones_row16_bf, b_ones16b),
                         (sb_onescol, ones_col_bf, b_onescol),
                         (sb_rv8, rowval8, b_rv8), (sb_ri16, rowidx16, b_ri16),
                         (consts, cvec, bconsts)):
        p.op(SYNC, dma(dst[:], src), w=[bb])

    stats = p.sb("stats", [128, 16], F32); bstats = Buf()
    scr = p.sb("scr", [128, D_l1ffn], F32); bscr = Buf()
    xt0 = p.sb("xt0", [128, D_l1ffn], F32); xt1 = xt0
    _bx = Buf(); bxt = [_bx, _bx]
    xn_tok0 = p.sb("xn_tok0", [128, D_l1ffn], BF16); xn_tok1 = p.sb("xn_tok1", [128, D_l1ffn], BF16)
    bxnt = [Buf(), Buf()]
    out_t = p.sb("out_t", [128, D_l1ffn], F32); bout_t = Buf()
    ln2o = p.sb("ln2o", [128, D_l1ffn], F32); bln2o = Buf()

    psP = [p.ps(f"psP{i}", [128, 512], F32) for i in range(4)]
    bpsP = [Buf() for _ in range(4)]
    psA, bpsA = psP[0][:, 0:512], bpsP[0]
    psB, bpsB = psP[1][:, 0:512], bpsP[1]
    psO = p.ps("psO", [128, 1024], F32); bpsO = Buf()
    psL = p.ps("psL", [128, 4], F32); bpsL = Buf()
    _psT = p.ps("psT0", [128, 128], BF16)
    _bpsT = Buf()
    psT2t = [_psT, _psT]
    bpsT2 = [_bpsT, _bpsT]
    psT, bpsT = _psT, _bpsT
    psTf = psP[2][:, 0:128]; bpsTf = bpsP[2]

    A_xn = p.sb("A_xn", [128, NKT_l1ffn * S_l1ffn], BF16); bA_xn = Buf()
    A_q = p.sb("A_q", [128, NKT_l1ffn * S_l1ffn], BF16); bA_q = Buf()
    A_k = p.sb("A_k", [128, NKT_l1ffn * S_l1ffn], BF16); bA_k = Buf()
    A_v = p.sb("A_v", [128, NTS_l1ffn * (D_l1ffn + 8)], BF16); bA_v = Buf()
    A_w1 = p.sb("A_w1", [128, NKT_l1ffn * D_l1ffn], BF16); bA_w1 = Buf()
    A_w2 = p.sb("A_w2", [128, NKT_l1ffn * D_l1ffn], BF16); bA_w2 = Buf()
    A_hm = p.sb("A_hm", [128, NKT_l1ffn * 512], BF16); bA_hm = Buf()

    A_ptc = A_xn[:, 0:NTS_l1ffn * 512]; bA_ptc = bA_xn
    nodeT = p.sb("nodeT", [128, NKT_l1ffn * 16], BF16); bnodeT = Buf()
    node_sb = p.sb("node_sb", [15, 1024], F32); bnode_sb = Buf()
    nbrow = p.sb("nbrow", [1, 16], F32); bnbrow = Buf()
    b1col = p.sb("b1col", [128, 8], F32); bb1 = Buf()
    b2sb = p.sb("b2sb", [16, 1024], BF16); bb2 = Buf()
    scores = p.sb("scores", [128, 256], F32); bscores = Buf()
    R = p.sb("R", [128, 16 * 12], F32); bR = Buf()
    lt = p.sb("lt", [128, 16], F32); blt = Buf()
    lrow = p.sb("lrow", [1, S_l1ffn], BF16); blrow = Buf()
    lbc = p.sb("lbc", [128, 512], BF16); blbc = Buf()
    lb16 = p.sb("lb16", [16, 512], BF16); blb16 = Buf()
    selT = p.sb("selT", [16, 512], BF16); bselT = Buf()
    hrelu = p.sb("hrelu", [128, 512], BF16); bhrelu = Buf()
    rl = p.sb("rl", [128, 1], F32); brl = Buf()

    def ln_stats(src_ap, bsrc):
        p.op(VECTOR, red(stats[:, 0:1], src_ap), r=[bsrc], w=[bstats])
        p.op(SCALAR, act(scr[:], src_ap, AF.Square, accum_out=stats[:, 1:2]),
             r=[bsrc], w=[bscr, bstats])
        p.op(VECTOR, ts(stats[:, 2:3], stats[:, 0:1], 1.0 / D_l1ffn), r=[bstats], w=[bstats])
        p.op(VECTOR, ts(stats[:, 3:4], stats[:, 2:3], -1.0), r=[bstats], w=[bstats])
        p.op(VECTOR, ts(stats[:, 4:5], stats[:, 1:2], 1.0 / D_l1ffn), r=[bstats], w=[bstats])
        p.op(VECTOR, stt(stats[:, 5:6], stats[:, 2:3], stats[:, 3:4], stats[:, 4:5],
                         ALU.mult, ALU.add), r=[bstats], w=[bstats])
        p.op(SCALAR, act(stats[:, 6:7], stats[:, 5:6], AF.Ln, bias=EPS),
             r=[bstats, bconsts], w=[bstats])
        p.op(SCALAR, act(stats[:, 7:8], stats[:, 6:7], AF.Exp, scale=NHALF),
             r=[bstats, bconsts], w=[bstats])

    for b in range(nseq):
        for t in range(NTS_l1ffn):
            xt = xt0 if t % 2 == 0 else xt1
            bx = bxt[t % 2]
            p.op(SYNC, dma(xt[:], x_in[b, t * 128:(t + 1) * 128, :]), w=[bx])
            ln_stats(xt[:], bx)
            xnt = xn_tok0 if t % 2 == 0 else xn_tok1
            bxn = bxnt[t % 2]
            p.op(VECTOR, ts(xnt[:], xt[:], stats[:, 2:3], stats[:, 7:8],
                            ALU.subtract, ALU.mult), r=[bx, bstats], w=[bxn])
            for kt in range(NKT_l1ffn):
                pt, bpt = psT2t[kt % 2], bpsT2[kt % 2]
                p.op(TENSOR, tp(pt[0:128, :], xnt[:, kt * 128:(kt + 1) * 128], sb_id[:]),
                     r=[bxn, b_id], w=[bpt])
                dst = A_xn[:, kt * S_l1ffn + t * 128: kt * S_l1ffn + (t + 1) * 128]
                if kt % 2 == 0:
                    p.op(VECTOR, tcopy(dst, pt[0:128, :]), r=[bpt], w=[bA_xn])
                else:
                    p.op(SCALAR, act(dst, pt[0:128, :], AF.Identity), r=[bpt], w=[bA_xn])

        for fi, pref in enumerate(["q", "k", "v"]):
            for kt in range(NKT_l1ffn):
                p.op(GPSIMD, dma(A_w1[:, kt * D_l1ffn:(kt + 1) * D_l1ffn].rearrange("p (l h) -> p l h", l=16),
                                 w[f"{pref}_w1"][:, kt * 128:(kt + 1) * 128, :]
                                 .rearrange("l pp h -> pp l h")),
                     w=[bA_w1])
            p.op(GPSIMD, dma(A_w2[:].rearrange("p (kt o) -> p kt o", kt=NKT_l1ffn),
                             w[f"{pref}_w2"].rearrange("l h o -> (l h) o")
                             .rearrange("(kt pp) o -> pp kt o", pp=128)),
                 w=[bA_w2])
            p.op(SYNC, dma(node_sb[:], w[f"{pref}_node_w"]), w=[bnode_sb])
            p.op(SYNC, dma(nbrow[0:1, 0:15],
                           w[f"{pref}_node_b"].rearrange("(one m) -> one m", one=1)),
                 w=[bnbrow])
            p.op(GPSIMD, dma(b1col[:], w[f"{pref}_b1"].rearrange("l h -> (l h)")
                             .rearrange("(s pp) -> pp s", pp=128),
                             allow_slow_non_contiguous=True), w=[bb1])
            p.op(GPSIMD, dma(b2sb[:], w[f"{pref}_b2"]), w=[bb2])
            for kt in range(NKT_l1ffn):
                p.op(TENSOR, tp(psTf[0:128, 0:15], node_sb[:, kt * 128:(kt + 1) * 128],
                                sb_idf[0:15, 0:15]), r=[bnode_sb, b_idf], w=[bpsTf])
                p.op(VECTOR, tcopy(nodeT[:, kt * 16:kt * 16 + 15], psTf[0:128, 0:15]),
                     r=[bpsTf], w=[bnodeT])

            for t in range(NTS_l1ffn):
                pspick, bps = (psA, bpsA) if t % 2 == 0 else (psB, bpsB)
                for kt in range(NKT_l1ffn):
                    p.op(TENSOR, mm(pspick[:, 0:15],
                                    A_xn[:, kt * S_l1ffn + t * 128: kt * S_l1ffn + (t + 1) * 128],
                                    nodeT[:, kt * 16:kt * 16 + 15],
                                    start=(kt == 0), stop=False),
                         r=[bA_xn, bnodeT], w=[bps])
                p.op(TENSOR, mm(pspick[:, 0:15], sb_onesr[:], nbrow[0:1, 0:15],
                                start=False, stop=True), r=[b_onesr, bnbrow], w=[bps])
                p.op(VECTOR, tcopy(scores[:, t * 16:t * 16 + 15], pspick[:, 0:15]),
                     r=[bps], w=[bscores])
            Sv = [scores[:, j:256:16] for j in range(15)]
            Rv = [R[:, j * 16:(j + 1) * 16] for j in range(12)]
            def TT(dst, a, bb_, op):
                p.op(VECTOR, tt(dst, a, bb_, op), r=[bscores, bR], w=[bR])
            def IS(dst, a):
                p.op(VECTOR, ts(dst, a, 0.0, None, ALU.is_ge), r=[bscores, bR], w=[bR])
            B0, B1, B2, B3 = Rv[0], Rv[1], Rv[2], Rv[3]
            T0, T1, T2, T3 = Rv[4], Rv[5], Rv[6], Rv[7]
            T4, T5 = Rv[8], Rv[9]
            IS(B0, Sv[0])
            TT(T0, Sv[2], Sv[1], ALU.subtract); TT(T0, T0, B0, ALU.mult)
            TT(T0, T0, Sv[1], ALU.add); IS(B1, T0)
            TT(T0, Sv[6], Sv[5], ALU.subtract); TT(T0, T0, B1, ALU.mult)
            TT(T0, T0, Sv[5], ALU.add)
            TT(T1, Sv[4], Sv[3], ALU.subtract); TT(T1, T1, B1, ALU.mult)
            TT(T1, T1, Sv[3], ALU.add)
            TT(T2, T0, T1, ALU.subtract); TT(T2, T2, B0, ALU.mult)
            TT(T2, T2, T1, ALU.add); IS(B2, T2)
            for i in range(4):
                TT(Rv[4 + i], Sv[8 + 2 * i], Sv[7 + 2 * i], ALU.subtract)
                TT(Rv[4 + i], Rv[4 + i], B2, ALU.mult)
                TT(Rv[4 + i], Rv[4 + i], Sv[7 + 2 * i], ALU.add)
            TT(Rv[8], Rv[5], Rv[4], ALU.subtract); TT(Rv[8], Rv[8], B1, ALU.mult)
            TT(Rv[8], Rv[8], Rv[4], ALU.add)
            TT(Rv[9], Rv[7], Rv[6], ALU.subtract); TT(Rv[9], Rv[9], B1, ALU.mult)
            TT(Rv[9], Rv[9], Rv[6], ALU.add)
            TT(Rv[10], Rv[9], Rv[8], ALU.subtract); TT(Rv[10], Rv[10], B0, ALU.mult)
            TT(Rv[10], Rv[10], Rv[8], ALU.add); IS(B3, Rv[10])
            p.op(VECTOR, ts(lt[:], B0, 8.0), r=[bR], w=[blt])
            p.op(VECTOR, stt(lt[:], B1, 4.0, lt[:], ALU.mult, ALU.add),
                 r=[bR, blt], w=[blt])
            p.op(VECTOR, stt(lt[:], B2, 2.0, lt[:], ALU.mult, ALU.add), r=[bR, blt], w=[blt])
            p.op(VECTOR, tt(lt[:], lt[:], B3, ALU.add), r=[bR, blt], w=[blt])
            for t in range(NTS_l1ffn):
                p.op(TENSOR, tp(psTf[0:1, 0:128], lt[:, t:t + 1], sb_idf[:]),
                     r=[blt, b_idf], w=[bpsTf])
                p.op(VECTOR, tcopy(lrow[0:1, t * 128:(t + 1) * 128], psTf[0:1, 0:128]),
                     r=[bpsTf], w=[blrow])

            for ch in range(NCH_l1ffn):
                p.op(TENSOR, mm(psA[:], sb_onesb[:], lrow[0:1, ch * 512:(ch + 1) * 512],
                                start=True, stop=True), r=[b_onesb, blrow], w=[bpsA])
                p.op(VECTOR, tcopy(lbc[:], psA[:]), r=[bpsA], w=[blbc])
                p.op(TENSOR, mm(psB[0:16, :], sb_ones16b[:], lrow[0:1, ch * 512:(ch + 1) * 512],
                                start=True, stop=True), r=[b_ones16b, blrow], w=[bpsB])
                p.op(VECTOR, tcopy(lb16[:], psB[0:16, :]), r=[bpsB], w=[blb16])
                p.op(VECTOR, ts(selT[:], lb16[:], sb_ri16[:, 0:1], None, ALU.is_equal),
                     r=[blb16, b_ri16], w=[bselT])
                for khg in range(2):
                    for kt in range(NKT_l1ffn):
                        for j in range(4):
                            kh = khg * 4 + j
                            p.op(TENSOR, mm(psP[j][:, 0:512],
                                            A_w1[:, kt * D_l1ffn + kh * 128: kt * D_l1ffn + (kh + 1) * 128],
                                            A_xn[:, kt * S_l1ffn + ch * 512: kt * S_l1ffn + (ch + 1) * 512],
                                            start=(kt == 0), stop=(kt == NKT_l1ffn - 1)),
                                 r=[bA_w1, bA_xn], w=[bpsP[j]])
                    for j in range(4):
                        kh = khg * 4 + j
                        p.op(SCALAR, act(hrelu[:], psP[j][:, 0:512], AF.Relu,
                                         bias=b1col[:, kh:kh + 1]),
                             r=[bpsP[j], bb1], w=[bhrelu])
                        p.op(VECTOR, stt(A_hm[:, kh * 512:(kh + 1) * 512], lbc[:],
                                         sb_rv8[:, kh:kh + 1], hrelu[:], ALU.is_equal,
                                         ALU.mult), r=[blbc, b_rv8, bhrelu], w=[bA_hm])
                if pref in ("q", "k"):
                    dstA, bdst = (A_q, bA_q) if pref == "q" else (A_k, bA_k)
                    for msg in range(2):
                        for kh in range(NKT_l1ffn):
                            for j in range(4):
                                ms = msg * 4 + j
                                p.op(TENSOR, mm(psP[j][:, 0:512],
                                                A_w2[:, kh * D_l1ffn + ms * 128: kh * D_l1ffn + (ms + 1) * 128],
                                                A_hm[:, kh * 512:(kh + 1) * 512],
                                                start=(kh == 0), stop=False),
                                     r=[bA_w2, bA_hm], w=[bpsP[j]])
                        for j in range(4):
                            ms = msg * 4 + j
                            p.op(TENSOR, mm(psP[j][:, 0:512], b2sb[:, ms * 128:(ms + 1) * 128],
                                            selT[:], start=False, stop=True),
                                 r=[bb2, bselT], w=[bpsP[j]])
                            if j % 2 == 0:
                                p.op(VECTOR, tcopy(dstA[:, ms * S_l1ffn + ch * 512: ms * S_l1ffn + (ch + 1) * 512],
                                                   psP[j][:, 0:512]), r=[bpsP[j]], w=[bdst])
                            else:
                                p.op(SCALAR, act(dstA[:, ms * S_l1ffn + ch * 512: ms * S_l1ffn + (ch + 1) * 512],
                                                 psP[j][:, 0:512], AF.Identity),
                                     r=[bpsP[j]], w=[bdst])
                else:
                    for tm in range(4):
                        t = ch * 4 + tm
                        vbase = t * (D_l1ffn + 8)
                        pb = 2 * (tm % 2)
                        p.op(VECTOR, memset(A_v[:, t * (D_l1ffn + 8) + D_l1ffn: (t + 1) * (D_l1ffn + 8)], 1.0),
                             w=[bA_v])
                        for kh in range(NKT_l1ffn):
                            for c2 in range(2):
                                p.op(TENSOR, mm(psP[pb + c2][:, 0:512],
                                                A_hm[:, kh * 512 + tm * 128: kh * 512 + (tm + 1) * 128],
                                                A_w2[:, kh * D_l1ffn + c2 * 512: kh * D_l1ffn + (c2 + 1) * 512],
                                                start=(kh == 0), stop=False),
                                     r=[bA_w2, bA_hm], w=[bpsP[pb + c2]])
                        for c2 in range(2):
                            p.op(TENSOR, mm(psP[pb + c2][:, 0:512],
                                            selT[:, tm * 128:(tm + 1) * 128],
                                            b2sb[:, c2 * 512:(c2 + 1) * 512],
                                            start=False, stop=True),
                                 r=[bselT, bb2], w=[bpsP[pb + c2]])
                            if c2 % 2 == 0:
                                p.op(VECTOR, tcopy(A_v[:, vbase + c2 * 512: vbase + (c2 + 1) * 512],
                                                   psP[pb + c2][:, 0:512]), r=[bpsP[pb + c2]], w=[bA_v])
                            else:
                                p.op(SCALAR, act(A_v[:, vbase + c2 * 512: vbase + (c2 + 1) * 512],
                                                 psP[pb + c2][:, 0:512], AF.Identity),
                                     r=[bpsP[pb + c2]], w=[bA_v])

        for qc in range(NCH_l1ffn):
            for ksg in range(4):
                for kt in range(NKT_l1ffn):
                    for j in range(4):
                        ks = ksg * 4 + j
                        p.op(TENSOR, mm(psP[j][:, 0:512],
                                        A_k[:, kt * S_l1ffn + ks * 128: kt * S_l1ffn + (ks + 1) * 128],
                                        A_q[:, kt * S_l1ffn + qc * 512: kt * S_l1ffn + (qc + 1) * 512],
                                        start=(kt == 0), stop=(kt == NKT_l1ffn - 1)),
                             r=[bA_k, bA_q], w=[bpsP[j]])
                for j in range(4):
                    ks = ksg * 4 + j
                    p.op(SCALAR, act(A_ptc[:, ks * 512:(ks + 1) * 512], psP[j][:, 0:512],
                                     AF.Exp, scale=INV32), r=[bpsP[j], bconsts], w=[bA_ptc])
            for qs in range(4):
                for ks in range(NTS_l1ffn):
                    lhs = A_ptc[:, ks * 512 + qs * 128: ks * 512 + (qs + 1) * 128]
                    p.op(TENSOR, mm(psO[:, 0:512], lhs,
                                    A_v[:, ks * (D_l1ffn + 8): ks * (D_l1ffn + 8) + 512],
                                    start=(ks == 0), stop=False),
                         r=[bA_ptc, bA_v], w=[bpsO])
                    p.op(TENSOR, mm(psO[:, 512:1024], lhs,
                                    A_v[:, ks * (D_l1ffn + 8) + 512: ks * (D_l1ffn + 8) + 1024],
                                    start=(ks == 0), stop=False),
                         r=[bA_ptc, bA_v], w=[bpsO])
                    p.op(TENSOR, mm(psL[:, 0:1], lhs,
                                    A_v[:, ks * (D_l1ffn + 8) + D_l1ffn: ks * (D_l1ffn + 8) + D_l1ffn + 1],
                                    start=(ks == 0), stop=(ks == NTS_l1ffn - 1)),
                         r=[bA_ptc, bA_v], w=[bpsL])
                p.op(VECTOR, recip(rl[:], psL[:, 0:1]), r=[bpsL], w=[brl])
                p.op(VECTOR, ts(ln2o[:, 0:512], psO[:, 0:512], rl[:, 0:1], None,
                                ALU.mult), r=[bpsO, brl], w=[bln2o])
                p.op(VECTOR, ts(ln2o[:, 512:1024], psO[:, 512:1024], rl[:, 0:1], None,
                                ALU.mult), r=[bpsO, brl], w=[bln2o])
                ln_stats(ln2o[:], bln2o)
                t = qc * 4 + qs
                xt = xt0 if t % 2 == 0 else xt1
                bx = bxt[t % 2]
                p.op(SYNC, dma(xt[:], x_in[b, t * 128:(t + 1) * 128, :]), w=[bx])
                p.op(VECTOR, ts(out_t[:], ln2o[:], stats[:, 2:3], stats[:, 7:8],
                                ALU.subtract, ALU.mult), r=[bln2o, bstats], w=[bout_t])
                p.op(VECTOR, tt(out_t[:], out_t[:], xt[:], ALU.add), r=[bout_t, bx],
                     w=[bout_t])
                p.op(SYNC, dma(x_out[b, t * 128:(t + 1) * 128, :], out_t[:]),
                     r=[bout_t], sem="store")
    p.op(SYNC, lambda e: None, deps=[("store", p.cnt.get("store", 0))])




from concourse.bass_utils import run_bass_kernel_spmd

NCORES = 8
NSEQ_PER_CORE = 2


def _shard_x(x):
    return [np.ascontiguousarray(x[i * NSEQ_PER_CORE:(i + 1) * NSEQ_PER_CORE])
            for i in range(NCORES)]


def kernel(**inp):
    x0 = inp["L0_attn_hidden_states"]
    shards = _shard_x(x0)
    wl0 = {n: inp["L0_attn_" + n] for n in
           ["Wq", "Wk", "Wv", "Wb", "Wdec", "Wmix", "mix_bias",
            "wq_conv", "wk_conv", "wv_conv", "onorm_w", "Wo"]}
    wffn = {n: inp["L0_ffn_" + n] for n in ["W_in", "b_in", "W_out", "b_out"]}
    wl1 = {n: inp["L1_attn_" + n] for n in ["c_attn_w", "c_attn_b", "c_proj_w", "c_proj_b"]}
    fnames = [f"{pref}_{s}" for pref in ("q", "k", "v")
              for s in ("node_w", "node_b", "w1", "b1", "w2", "b2")]
    wlf = {n: inp["L1_ffn_" + n] for n in fnames}

    p = Prog("lA")
    x_in = p.dram_in("x", [NSEQ_PER_CORE, 2048, 1024], F32)
    x1_out = p.dram_out("x1", [NSEQ_PER_CORE, 2048, 1024], F32)
    out_xmax = p.dram_out("xmax", [1, 16], F32)
    waps0 = {n: p.dram_in("a_" + n, v.shape, F32) for n, v in wl0.items()}
    build_l0_attn(p, x_in, x1_out, waps0, nseq=NSEQ_PER_CORE, out_xmax=out_xmax)
    p.finalize()
    res = run_bass_kernel_spmd(
        p.nc,
        p.in_maps([{**{"a_" + n: v for n, v in wl0.items()}, "x": s} for s in shards]),
        core_ids=list(range(NCORES)))
    x1 = [res.results[i]["x1"] for i in range(NCORES)]
    xmax8 = np.concatenate([res.results[i]["xmax"] for i in range(NCORES)], axis=0)
    p.close()

    p = Prog("lB")
    x_in = p.dram_in("x", [NSEQ_PER_CORE, 2048, 1024], F32)
    wapsF = {n: p.dram_in("f_" + n, v.shape, F32) for n, v in wffn.items()}
    xmax_in = p.dram_in("xmax8", [NCORES, 16], F32)
    out_hmax = p.dram_out("hmax", [1, 16], F32)
    out_hid = p.dram_out("hid", [4096, NSEQ_PER_CORE * 2048], BF16)
    build_l0_ffn(p, x_in, wapsF, nseq=NSEQ_PER_CORE, mode="amax_h",
                 xmax_in=xmax_in, nxmax=NCORES, out_hmax=out_hmax,
                 out_hid=out_hid)
    p.finalize()
    res = run_bass_kernel_spmd(
        p.nc,
        p.in_maps([{**{"f_" + n: v for n, v in wffn.items()},
                    "x": s, "xmax8": xmax8} for s in x1]),
        core_ids=list(range(NCORES)))
    hmax8 = np.concatenate([res.results[i]["hmax"] for i in range(NCORES)], axis=0)
    hid = [res.results[i]["hid"] for i in range(NCORES)]
    p.close()

    p = Prog("lC")
    x_in = p.dram_in("x", [NSEQ_PER_CORE, 2048, 1024], F32)
    wapsF = {n: p.dram_in("f_" + n, v.shape, F32) for n, v in wffn.items()}
    waps1 = {n: p.dram_in("g_" + n, v.shape, F32) for n, v in wl1.items()}
    wapsf = {n: p.dram_in("h_" + n, v.shape, F32) for n, v in wlf.items()}
    xmax_in = p.dram_in("xmax8", [NCORES, 16], F32)
    hmax_in = p.dram_in("hmax8", [NCORES, 16], F32)
    hid_in = p.dram_in("hid", [4096, NSEQ_PER_CORE * 2048], BF16)
    x2 = p.dram_tmp("x2", [NSEQ_PER_CORE, 2048, 1024], F32)
    x3 = p.dram_tmp("x3", [NSEQ_PER_CORE, 2048, 1024], F32)
    x4 = p.dram_out("x_new", [NSEQ_PER_CORE, 2048, 1024], F32)
    p.push_scope()
    build_l0_ffn(p, x_in, wapsF, nseq=NSEQ_PER_CORE, mode="full",
                 xmax_in=xmax_in, nxmax=NCORES, hmax_in=hmax_in, nhmax=NCORES,
                 x_out=x2, hid_in=hid_in)
    p.pop_scope()
    p.push_scope()
    build_l1_attn(p, x2, x3, waps1, nseq=NSEQ_PER_CORE)
    p.pop_scope()
    p.push_scope()
    build_l1_ffn(p, x3, x4, wapsf, nseq=NSEQ_PER_CORE)
    p.pop_scope()
    p.finalize()
    res = run_bass_kernel_spmd(
        p.nc,
        p.in_maps([{**{"f_" + n: v for n, v in wffn.items()},
                    **{"g_" + n: v for n, v in wl1.items()},
                    **{"h_" + n: v for n, v in wlf.items()},
                    "x": s, "xmax8": xmax8, "hmax8": hmax8,
                    "hid": hid[i]} for i, s in enumerate(x1)]),
        core_ids=list(range(NCORES)))
    out = np.concatenate([res.results[i]["x_new"] for i in range(NCORES)], axis=0)
    p.close()
    return out.astype(np.float32)
